# version 29
# baseline (speedup 1.0000x reference)
import numpy as np
import ml_dtypes
import concourse.bass as bass
import concourse.mybir as mybir
from concourse.bass_utils import run_bass_kernel_spmd
from contextlib import ExitStack

F32 = mybir.dt.float32
BF16 = mybir.dt.bfloat16
I32 = mybir.dt.int32
Act = mybir.ActivationFunctionType
Alu = mybir.AluOpType
AX = mybir.AxisListType.X

NCORE = 8
T = 4096
C = 1024
HID = 3072
G = 8
HG = 24
TC = 512
NCH = T // TC
D = 128
P_INIT = -1.0e30
BK = 512


def build_nc():
    nc = bass.Bass("TRN2", target_bir_lowering=False)

    x_hbm = nc.dram_tensor("x_cm", [C, T], F32, kind="ExternalInput").ap()
    wk_hbm = nc.dram_tensor("wkT", [C, C], F32, kind="ExternalInput").ap()
    wr_hbm = nc.dram_tensor("wrT", [C, C], F32, kind="ExternalInput").ap()
    wo_hbm = nc.dram_tensor("woT", [C, C], F32, kind="ExternalInput").ap()
    td_hbm = nc.dram_tensor("td_in", [D, G], F32, kind="ExternalInput").ap()
    tf_hbm = nc.dram_tensor("tf_in", [D, G], F32, kind="ExternalInput").ap()
    wg_hbm = nc.dram_tensor("wgT", [C, HID], I32, kind="ExternalInput").ap()
    wu_hbm = nc.dram_tensor("wuT", [C, HID], I32, kind="ExternalInput").ap()
    wd_hbm = nc.dram_tensor("wdT", [HID, C], I32, kind="ExternalInput").ap()
    out_hbm = nc.dram_tensor("out_cm", [C, T], F32, kind="ExternalOutput").ap()
    wg_bfd = nc.dram_tensor("wg_bfd", [HG, D, C], BF16, kind="Internal").ap()
    wu_bfd = nc.dram_tensor("wu_bfd", [HG, D, C], BF16, kind="Internal").ap()
    wd_bfd = nc.dram_tensor("wd_bfd", [HID, C], BF16, kind="Internal").ap()

    def hbm_chunk(ap_, j):
        return ap_[:, j * TC:(j + 1) * TC].rearrange("(g p) t -> p g t", p=D)

    def hbm_half(ap_, j, h):
        return ap_[h * 512:(h + 1) * 512,
                   j * TC:(j + 1) * TC].rearrange("(g p) t -> p g t", p=D)

    es = ExitStack()
    with es:
        sem = {}
        for name in [
            "sPio", "g_wk", "g_wr", "g_wo", "s_pexp", "v_pw", "s_pew",
            "sA_xa0", "sA_xa1", "sA_xb0", "sA_xb1", "v_sqa", "v_sqb",
            "t_stats", "v_var", "s_srow", "v_sv",
            "t_bcs", "s_sev", "t_bcm", "s_msev", "v_xn",
            "t_kog", "t_rog", "v_mx", "s_mx", "s_rev", "s_ekd", "v_m", "s_es",
            "s_act", "v_den", "s_rec", "v_out", "t_wo", "v_res",
            "g_wgt0", "g_wgt1", "g_wut0", "g_wut1", "g_wdt0", "g_wdt1",
            "g_fcw1", "g_fcw2", "g_fcu1", "g_fcu2", "g_fcd",
            "t_g", "t_u", "t_d1", "s_sil", "s_upev",
            "v_hact", "t_p1", "v_res1", "t_p2", "v_res2",
            "g_oa0", "g_oa1", "g_ob0", "g_ob1",
        ]:
            sem[name] = es.enter_context(nc.semaphore(name))

        def sa_xa(j):
            return sem[f"sA_xa{j % 2}"], 16 * (j // 2 + 1)

        def sa_xb(j):
            return sem[f"sA_xb{j % 2}"], 16 * (j // 2 + 1)

        def g_oa(j):
            return sem[f"g_oa{j % 2}"], 16 * (j // 2 + 1)

        def g_ob(j):
            return sem[f"g_ob{j % 2}"], 16 * (j // 2 + 1)

        def tile_done(fam, ti):
            return sem[f"g_{fam}{ti % 2}"], 16 * (ti // 2 + 1)

        sA = ExitStack()
        tdt = sA.enter_context(nc.sbuf_tensor("tdt", [D, G], F32)).ap()
        tft = sA.enter_context(nc.sbuf_tensor("tft", [D, G], F32)).ap()
        exptd = sA.enter_context(nc.sbuf_tensor("exptd", [D, G], F32)).ap()
        wv = sA.enter_context(nc.sbuf_tensor("wv", [D, G], F32)).ap()
        ewv = sA.enter_context(nc.sbuf_tensor("ewv", [D, G], F32)).ap()
        euv = sA.enter_context(nc.sbuf_tensor("euv", [D, G], F32)).ap()
        relu_u = sA.enter_context(nc.sbuf_tensor("relu_u", [D, G], F32)).ap()
        a_all = sA.enter_context(nc.sbuf_tensor("a_all", [D, G], F32)).ap()
        b_all = sA.enter_context(nc.sbuf_tensor("b_all", [D, G], F32)).ap()
        Mall = sA.enter_context(nc.sbuf_tensor("Mall", [D, G], F32)).ap()
        t1b = sA.enter_context(nc.sbuf_tensor("t1b", [D, G], F32)).ap()
        m_all2 = [sA.enter_context(nc.sbuf_tensor(f"m_all{i}", [D, G], F32)).ap() for i in range(2)]
        neg_m2 = [sA.enter_context(nc.sbuf_tensor(f"neg_m{i}", [D, G], F32)).ap() for i in range(2)]
        dp2 = [sA.enter_context(nc.sbuf_tensor(f"dp{i}", [D, G], F32)).ap() for i in range(2)]
        esb2 = [sA.enter_context(nc.sbuf_tensor(f"esb{i}", [D, G], F32)).ap() for i in range(2)]
        a02 = [sA.enter_context(nc.sbuf_tensor(f"a0{i}", [D, G], F32)).ap() for i in range(2)]
        b02 = [sA.enter_context(nc.sbuf_tensor(f"b0{i}", [D, G], F32)).ap() for i in range(2)]
        onescol_bf = sA.enter_context(nc.sbuf_tensor("onescol_bf", [D, 1], BF16)).ap()
        onescol_f = sA.enter_context(nc.sbuf_tensor("onescol_f", [D, 1], F32)).ap()
        eps_t = sA.enter_context(nc.sbuf_tensor("eps_t", [D, 1], F32)).ap()
        nc.const_aps.register(F32, 1e-5, eps_t)
        ones1 = sA.enter_context(nc.sbuf_tensor("ones1", [1, D], F32)).ap()
        ew_1 = sA.enter_context(nc.sbuf_tensor("ew_1", [D, TC], F32)).ap()
        ones_tc = sA.enter_context(nc.sbuf_tensor("ones_tc", [D, TC], F32)).ap()
        wk_bf = sA.enter_context(nc.sbuf_tensor("wk_bf", [D, G * 1024], BF16)).ap()
        wr_bf = sA.enter_context(nc.sbuf_tensor("wr_bf", [D, G * 1024], BF16)).ap()
        wo_bf = sA.enter_context(nc.sbuf_tensor("wo_bf", [D, G * 1024], BF16)).ap()
        x_buf = [sA.enter_context(nc.sbuf_tensor(f"x_buf{i}", [D, G * TC], F32)).ap()
                 for i in range(2)]
        xn_buf = [sA.enter_context(nc.sbuf_tensor(f"xn_buf{i}", [D, G * TC], BF16)).ap()
                  for i in range(2)]
        r_sb = sA.enter_context(nc.sbuf_tensor("r_sb", [D, G * TC], F32)).ap()
        outpre = sA.enter_context(nc.sbuf_tensor("outpre", [D, G * TC], BF16)).ap()
        rows = sA.enter_context(nc.sbuf_tensor("rows", [1, 3 * TC], F32)).ap()
        m_row = rows[0:1, 0:TC]
        msq_row = rows[0:1, TC:2 * TC]
        s_row = rows[0:1, 2 * TC:3 * TC]
        ms_row = m_row
        var_row = s_row
        sb_sb = sA.enter_context(nc.sbuf_tensor("sb_sb", [D, TC], F32)).ap()
        msb_sb = sA.enter_context(nc.sbuf_tensor("msb_sb", [D, TC], F32)).ap()
        ek_s = [sA.enter_context(nc.sbuf_tensor(f"ek{i}", [D, TC], F32)).ap() for i in range(3)]
        enr_s = [sA.enter_context(nc.sbuf_tensor(f"enr{i}", [D, TC], F32)).ap() for i in range(3)]
        zA_s = [sA.enter_context(nc.sbuf_tensor(f"zA{i}", [D, TC], F32)).ap() for i in range(2)]
        eu_1 = sA.enter_context(nc.sbuf_tensor("eu_1", [D, TC], F32)).ap()
        den_1 = sA.enter_context(nc.sbuf_tensor("den_1", [D, TC], F32)).ap()
        SA_e = [sA.enter_context(nc.sbuf_tensor(f"SA{i}", [D, TC + 4], F32)).ap() for i in range(2)]
        SB_e = [sA.enter_context(nc.sbuf_tensor(f"SB{i}", [D, TC + 4], F32)).ap() for i in range(2)]
        h_act = sA.enter_context(nc.sbuf_tensor("h_act", [D, HG * TC], BF16)).ap()
        silu_sb = [sA.enter_context(nc.sbuf_tensor(f"silu{i}", [D, TC], F32)).ap() for i in range(2)]
        up_sb = [sA.enter_context(nc.sbuf_tensor(f"up{i}", [D, TC], F32)).ap() for i in range(2)]
        wgt = [sA.enter_context(nc.sbuf_tensor(f"wgt{i}", [D, C], BF16)).ap() for i in range(2)]
        wut = [sA.enter_context(nc.sbuf_tensor(f"wut{i}", [D, C], BF16)).ap() for i in range(2)]
        wdt = [sA.enter_context(nc.sbuf_tensor(f"wdt{i}", [D, C], BF16)).ap() for i in range(2)]
        ps = sA.enter_context(nc.psum_tensor("ps", [D, 8 * BK], F32)).ap()

        def bank(i, w=BK):
            return ps[:, i * BK:i * BK + w]

        ps_stats = ps
        ps_bc = bank(1)
        ps_k = bank(4)
        ps_r = bank(5)

        def ps_wo(cg):
            return bank(5)

        ps_gate = bank(6)
        ps_up = bank(7)

        def ps_dn(dg):
            return bank(dg % 4)

        def gsl(ap_, g, w=TC):
            return ap_[:, g * w:(g + 1) * w]

        with nc.Block() as blk:

            @blk.sync
            def _(sy):
                sy.dma_start(out=tdt, in_=td_hbm).then_inc(sem["sPio"], 16)
                sy.dma_start(out=tft, in_=tf_hbm).then_inc(sem["sPio"], 16)
                for jj in range(2):
                    sy.dma_start(
                        out=x_buf[jj][:, 0:4 * TC].rearrange("p (g t) -> p g t", t=TC),
                        in_=hbm_half(x_hbm, jj, 0)).then_inc(sem[f"sA_xa{jj % 2}"], 16)
                    sy.dma_start(
                        out=x_buf[jj][:, 4 * TC:8 * TC].rearrange("p (g t) -> p g t", t=TC),
                        in_=hbm_half(x_hbm, jj, 1)).then_inc(sem[f"sA_xb{jj % 2}"], 16)
                for j in range(NCH):
                    for hg in range(HG):
                        ti = 24 * j + hg
                        if j == 0:
                            sy.wait_ge(sem["g_fcw1" if hg < 12 else "g_fcw2"], 16 * 12)
                        if ti >= 2:
                            sy.wait_ge(sem["t_g"], ti - 1)
                        sy.dma_start(out=wgt[ti % 2],
                                     in_=wg_bfd[hg]).then_inc(sem[f"g_wgt{ti % 2}"], 16)
                        if j == 0:
                            sy.wait_ge(sem["g_fcu1" if hg < 12 else "g_fcu2"], 16 * 12)
                        if ti >= 2:
                            sy.wait_ge(sem["t_u"], ti - 1)
                        sy.dma_start(out=wut[ti % 2],
                                     in_=wu_bfd[hg]).then_inc(sem[f"g_wut{ti % 2}"], 16)
                    for hg2 in range(2 * HG):
                        ti48 = 48 * j + hg2
                        if j == 0 and hg2 == 0:
                            sy.wait_ge(sem["g_fcd"], 16 * 4)
                        if ti48 >= 2:
                            sy.wait_ge(sem["t_d1"], ti48 - 1)
                        hgw = hg2 % HG
                        sy.dma_start(out=wdt[ti48 % 2],
                                     in_=wd_bfd[hgw * D:(hgw + 1) * D, :],
                                     ).then_inc(sem[f"g_wdt{ti48 % 2}"], 16)

            @blk.gpsimd
            def _(gp):
                gp.wait_ge(sem["sA_xa0"], 16)
                gp.dma_start(
                    out=wk_bf.rearrange("p (g n) -> p g n", n=1024),
                    in_=wk_hbm.rearrange("(g p) n -> p g n", p=D)).then_inc(sem["g_wk"], 16)
                gp.dma_start(
                    out=wr_bf.rearrange("p (g n) -> p g n", n=1024),
                    in_=wr_hbm.rearrange("(g p) n -> p g n", p=D)).then_inc(sem["g_wr"], 16)
                gp.dma_start(
                    out=wo_bf.rearrange("p (g n) -> p g n", n=1024),
                    in_=wo_hbm.rearrange("(g p) n -> p g n", p=D)).then_inc(sem["g_wo"], 16)
                fpieces = []
                for hg in range(HG):
                    fpieces.append((
                        wg_bfd[hg].rearrange("p (g h) -> p g h", h=D),
                        wg_hbm[:, hg * D:(hg + 1) * D].rearrange("(g p) h -> p g h", p=D),
                        "g_fcw1" if hg < 12 else "g_fcw2"))
                for hg in range(HG):
                    fpieces.append((
                        wu_bfd[hg].rearrange("p (g h) -> p g h", h=D),
                        wu_hbm[:, hg * D:(hg + 1) * D].rearrange("(g p) h -> p g h", p=D),
                        "g_fcu1" if hg < 12 else "g_fcu2"))
                for rr in range(4):
                    fpieces.append((wd_bfd[rr * 768:(rr + 1) * 768, :],
                                    wd_hbm[rr * 768:(rr + 1) * 768, :], "g_fcd"))
                for pc in range(len(fpieces)):
                    gp.dma_start(out=fpieces[pc][0],
                                 in_=fpieces[pc][1]).then_inc(sem[fpieces[pc][2]], 16)
                for fch in range(NCH):
                    gp.wait_ge(sem["v_res1"], fch + 1)
                    gp.dma_start(
                        out=hbm_half(out_hbm, fch, 0),
                        in_=x_buf[fch % 2][:, 0:4 * TC].rearrange(
                            "p (g t) -> p g t", t=TC),
                    ).then_inc(sem[f"g_oa{fch % 2}"], 16)
                    gp.wait_ge(sem["v_res2"], fch + 1)
                    gp.dma_start(
                        out=hbm_half(out_hbm, fch, 1),
                        in_=x_buf[fch % 2][:, 4 * TC:8 * TC].rearrange(
                            "p (g t) -> p g t", t=TC),
                    ).then_inc(sem[f"g_ob{fch % 2}"], 16)
                gp.wait_ge(sem["g_oa0"], 16 * (NCH // 2))
                gp.wait_ge(sem["g_oa1"], 16 * (NCH // 2))
                gp.wait_ge(sem["g_ob0"], 16 * (NCH // 2))
                gp.wait_ge(sem["g_ob1"], 16 * (NCH // 2))

            @blk.scalar
            def _(sc):
                sc.wait_ge(sem["sPio"], 32)
                sc.activation(out=exptd, in_=tdt, func=Act.Exp).then_inc(sem["s_pexp"], 1)
                sc.wait_ge(sem["v_pw"], 1)
                sc.activation(out=ewv, in_=wv, func=Act.Exp)
                sc.activation(out=euv, in_=tft,
                              func=Act.Exp).then_inc(sem["s_pew"], 1)
                sc.wait_ge(sem["v_m"], 1)
                sc.activation(out=esb2[0], in_=dp2[0], func=Act.Exp).then_inc(sem["s_es"], 1)
                sc.wait_ge(sem["v_var"], 1)
                sc.activation(out=msq_row, in_=var_row, func=Act.Ln, bias=1e-5)
                sc.drain()
                sc.activation(out=s_row, in_=msq_row, func=Act.Exp,
                              scale=-0.5).then_inc(sem["s_srow"], 1)
                sc.wait_ge(sem["t_bcs"], 1)
                sc.copy(out=sb_sb, in_=ps_bc).then_inc(sem["s_sev"], 1)
                sc.wait_ge(sem["t_bcm"], 1)
                sc.copy(out=msb_sb, in_=ps_bc).then_inc(sem["s_msev"], 1)

                for j in range(NCH + 1):
                    q = j % 2
                    f = j - 1

                    def silu_up(hg):
                        ti = 24 * f + hg
                        if ti >= 2:
                            sc.wait_ge(sem["v_hact"], ti - 1)
                        sc.wait_ge(sem["t_g"], ti + 1)
                        sc.activation(out=silu_sb[hg % 2], in_=ps_gate, func=Act.Silu,
                                      scale=1.0 / 127.0).then_inc(sem["s_sil"], 1)
                        sc.wait_ge(sem["t_u"], ti + 1)
                        sc.copy(out=up_sb[hg % 2], in_=ps_up).then_inc(sem["s_upev"], 1)

                    if j == NCH:
                        for hg in range(HG):
                            silu_up(hg)
                        break

                    for og in range(G):
                        idx = 8 * j + og
                        if f >= 0 and og >= 2:
                            silu_up(og - 2)
                        if idx >= 3:
                            sc.wait_ge(sem["v_out"], idx - 2)
                            sc.wait_ge(sem["v_den"], idx - 2)
                        sc.wait_ge(sem["t_kog"], idx + 1)
                        sc.activation(out=ek_s[idx % 3], in_=ps_k, func=Act.Exp,
                                      bias=neg_m2[q][:, og:og + 1]).then_inc(sem["s_ekd"], 1)
                        if j >= 1:
                            sc.wait_ge(sem["v_den"], 8 * (j - 1) + og + 1)
                        sc.wait_ge(sem["t_rog"], idx + 1)
                        sc.copy(out=gsl(r_sb, og), in_=ps_r).then_inc(sem["s_rev"], 1)
                        sc.drain()
                        sc.activation(out=enr_s[idx % 3], in_=gsl(r_sb, og), func=Act.Exp,
                                      scale=-1.0).then_inc(sem["s_act"], 1)
                        if og >= 1:
                            pidx = idx - 1
                            sc.wait_ge(sem["v_den"], pidx + 1)
                            sc.activation(out=eu_1, in_=den_1, func=Act.Ln)
                            sc.drain()
                            sc.activation(out=ek_s[pidx % 3], in_=eu_1, func=Act.Exp,
                                          scale=-1.0).then_inc(sem["s_rec"], 1)
                    if f >= 0:
                        for hg in range(6, 10):
                            silu_up(hg)
                    pidx = 8 * j + 7
                    sc.wait_ge(sem["v_den"], pidx + 1)
                    sc.activation(out=eu_1, in_=den_1, func=Act.Ln)
                    sc.drain()
                    sc.activation(out=ek_s[pidx % 3], in_=eu_1, func=Act.Exp,
                                  scale=-1.0).then_inc(sem["s_rec"], 1)
                    if f >= 0:
                        for hg in range(10, 24):
                            silu_up(hg)
                    if j >= 1 and j + 1 < NCH:
                        sc.wait_ge(*g_oa(j - 1))
                        sc.dma_start(
                            out=x_buf[(j + 1) % 2][:, 0:4 * TC].rearrange(
                                "p (g t) -> p g t", t=TC),
                            in_=hbm_half(x_hbm, j + 1, 0),
                        ).then_inc(sem[f"sA_xa{(j + 1) % 2}"], 16)
                        sc.wait_ge(*g_ob(j - 1))
                        sc.dma_start(
                            out=x_buf[(j + 1) % 2][:, 4 * TC:8 * TC].rearrange(
                                "p (g t) -> p g t", t=TC),
                            in_=hbm_half(x_hbm, j + 1, 1),
                        ).then_inc(sem[f"sA_xb{(j + 1) % 2}"], 16)
                    if j + 1 < NCH:
                        sc.wait_ge(sem["v_var"], j + 2)
                        sc.activation(out=msq_row, in_=var_row, func=Act.Ln, bias=1e-5)
                        sc.drain()
                        sc.activation(out=s_row, in_=msq_row, func=Act.Exp,
                                      scale=-0.5).then_inc(sem["s_srow"], 1)
                    if j + 1 < NCH:
                        sc.wait_ge(sem["v_mx"], j + 1)
                        sc.activation(out=t1b, in_=Mall,
                                      func=Act.Ln).then_inc(sem["s_mx"], 1)
                    if j + 1 < NCH:
                        q1 = (j + 1) % 2
                        sc.wait_ge(sem["v_m"], j + 2)
                        sc.activation(out=esb2[q1], in_=dp2[q1],
                                      func=Act.Exp).then_inc(sem["s_es"], 1)
                        sc.wait_ge(sem["t_bcs"], j + 2)
                        sc.copy(out=sb_sb, in_=ps_bc).then_inc(sem["s_sev"], 1)
                        sc.wait_ge(sem["t_bcm"], j + 2)
                        sc.copy(out=msb_sb, in_=ps_bc).then_inc(sem["s_msev"], 1)

            @blk.vector
            def _(ve):
                ve.memset(onescol_bf, 1.0)
                ve.memset(onescol_f, 1.0)
                ve.memset(eps_t, 1e-5)
                ve.memset(ones1, 1.0)
                ve.memset(a_all, 0.0)
                ve.memset(b_all, 0.0)
                ve.memset(Mall, 0.0)
                ve.memset(m_all2[1], P_INIT)
                ve.memset(den_1, 1.0)
                ve.drain()
                ve.wait_ge(sem["s_pexp"], 1)
                ve.tensor_scalar_mul(wv, exptd, -1.0).then_inc(sem["v_pw"], 1)
                ve.drain()
                ve.memset(ones_tc, 1.0)
                ve.wait_ge(sem["sPio"], 32)
                ve.tensor_scalar_max(relu_u, tft, 0.0)
                ve.drain()
                ve.tensor_add(t1b, Mall, relu_u)
                ve.drain()
                ve.tensor_max(m_all2[0], t1b, m_all2[1])
                ve.drain()
                ve.tensor_scalar_mul(neg_m2[0], m_all2[0], -1.0)
                ve.tensor_sub(dp2[0], m_all2[1], m_all2[0]).then_inc(sem["v_m"], 1)
                ve.wait_ge(*sa_xa(0))
                ve.tensor_mul(xn_buf[0][:, 0:4 * TC], x_buf[0][:, 0:4 * TC],
                              x_buf[0][:, 0:4 * TC]).then_inc(sem["v_sqa"], 1)
                ve.wait_ge(*sa_xb(0))
                ve.tensor_mul(xn_buf[0][:, 4 * TC:8 * TC], x_buf[0][:, 4 * TC:8 * TC],
                              x_buf[0][:, 4 * TC:8 * TC]).then_inc(sem["v_sqb"], 1)
                ve.wait_ge(sem["s_es"], 1)
                ve.tensor_mul(a02[0], a_all, esb2[0])
                ve.tensor_mul(b02[0], b_all, esb2[0])
                ve.wait_ge(sem["t_stats"], 1)
                ve.tensor_scalar_mul(m_row, ps_stats[0:1, 0:TC], 1.0 / C)
                ve.tensor_scalar_mul(var_row, ps_stats[32:33, 0:TC], 1.0 / C)
                ve.drain()
                ve.tensor_mul(msq_row, m_row, m_row)
                ve.drain()
                ve.scalar_tensor_tensor(
                    out=var_row, in0=msq_row, scalar=-1.0, in1=var_row,
                    op0=Alu.mult, op1=Alu.add).then_inc(sem["v_var"], 1)
                ve.wait_ge(sem["s_srow"], 1)
                ve.tensor_mul(ms_row, m_row, s_row).then_inc(sem["v_sv"], 1)
                xtmp = [zA_s[0], zA_s[1], den_1, eu_1,
                        SA_e[0][:, 0:TC], SA_e[1][:, 0:TC],
                        SB_e[0][:, 0:TC], SB_e[1][:, 0:TC]]
                ve.wait_ge(sem["s_msev"], 1)
                for g in range(G):
                    ve.tensor_mul(xtmp[g], gsl(x_buf[0], g), sb_sb)
                ve.drain()
                for g in range(G - 1):
                    ve.tensor_sub(gsl(xn_buf[0], g), xtmp[g], msb_sb)
                ve.tensor_sub(gsl(xn_buf[0], G - 1), xtmp[G - 1],
                              msb_sb).then_inc(sem["v_xn"], 1)
                ve.wait_ge(*sa_xa(1))
                ve.tensor_mul(xn_buf[1][:, 0:4 * TC], x_buf[1][:, 0:4 * TC],
                              x_buf[1][:, 0:4 * TC]).then_inc(sem["v_sqa"], 1)
                ve.wait_ge(*sa_xb(1))
                ve.tensor_mul(xn_buf[1][:, 4 * TC:8 * TC], x_buf[1][:, 4 * TC:8 * TC],
                              x_buf[1][:, 4 * TC:8 * TC]).then_inc(sem["v_sqb"], 1)

                for j in range(NCH + 1):
                    q = j % 2
                    q1 = (j + 1) % 2
                    f = j - 1

                    def outpre_g(g):
                        idx = 8 * j + g
                        pg = g % 2
                        if g == 0:
                            ve.wait_ge(sem["t_wo"], 8 * j)
                        ve.wait_ge(sem["s_rec"], idx + 1)
                        ve.tensor_mul(gsl(outpre, g), zA_s[pg],
                                      ek_s[idx % 3]).then_inc(sem["v_out"], 1)

                    def scan_group(g):
                        idx = 8 * j + g
                        pg = g % 2
                        ve.wait_ge(sem["s_act"], idx + 1)
                        if g >= 2:
                            outpre_g(g - 2)
                        ve.tensor_copy(SA_e[pg][:, 0:1], a02[q][:, g:g + 1])
                        ve.tensor_copy(SB_e[pg][:, 0:1], b02[q][:, g:g + 1])
                        if idx == 0:
                            ve.wait_ge(sem["s_pew"], 1)
                        ve.tensor_scalar_mul(ew_1, ones_tc, ewv[:, g:g + 1])
                        ve.drain()
                        ve.tensor_mul(zA_s[pg], ek_s[idx % 3], gsl(r_sb, g))
                        ve.reduce_max(Mall[:, g:g + 1], ek_s[idx % 3], axis=AX)
                        ve.drain()
                        ve.tensor_tensor_scan(
                            out=SA_e[pg][:, 1:TC + 1], data0=ew_1,
                            data1=zA_s[pg], initial=a02[q][:, g:g + 1],
                            op0=Alu.mult, op1=Alu.add)
                        ve.tensor_tensor_scan(
                            out=SB_e[pg][:, 1:TC + 1], data0=ew_1,
                            data1=ek_s[idx % 3], initial=b02[q][:, g:g + 1],
                            op0=Alu.mult, op1=Alu.add)
                        ve.drain()
                        ve.scalar_tensor_tensor(
                            out=zA_s[pg], in0=zA_s[pg], scalar=euv[:, g:g + 1],
                            in1=SA_e[pg][:, 0:TC], op0=Alu.mult, op1=Alu.add)
                        if idx >= 1:
                            ve.wait_ge(sem["s_rec"], idx)
                        ve.scalar_tensor_tensor(
                            out=den_1, in0=ek_s[idx % 3], scalar=euv[:, g:g + 1],
                            in1=SB_e[pg][:, 0:TC], op0=Alu.mult, op1=Alu.add)
                        ve.tensor_copy(a_all[:, g:g + 1], SA_e[pg][:, TC:TC + 1])
                        ve.tensor_copy(b_all[:, g:g + 1], SB_e[pg][:, TC:TC + 1])
                        ve.drain()
                        ve.scalar_tensor_tensor(
                            out=den_1, in0=enr_s[idx % 3], scalar=1.0,
                            in1=den_1, op0=Alu.add,
                            op1=Alu.mult).then_inc(sem["v_den"], 1)

                    def h_act_op(hg):
                        ti = 24 * f + hg
                        ve.wait_ge(sem["s_sil"], ti + 1)
                        ve.wait_ge(sem["s_upev"], ti + 1)
                        ve.scalar_tensor_tensor(
                            out=h_act[:, hg * TC:(hg + 1) * TC], in0=up_sb[hg % 2],
                            scalar=1.0 / 127.0, in1=silu_sb[hg % 2], op0=Alu.mult,
                            op1=Alu.mult).then_inc(sem["v_hact"], 1)

                    def resid_cg(cg):
                        ve.wait_ge(sem["t_wo"], 8 * j + cg + 1)
                        ve.tensor_add(gsl(x_buf[j % 2], cg),
                                      gsl(x_buf[j % 2], cg),
                                      ps_wo(cg)).then_inc(sem["v_res"], 1)

                    def ffn_res1():
                        ve.wait_ge(sem["t_d1"], 48 * f + 24)
                        for dg in range(4):
                            ins = ve.scalar_tensor_tensor(
                                out=gsl(x_buf[f % 2], dg), in0=ps_dn(dg),
                                scalar=1.0 / 127.0, in1=gsl(x_buf[f % 2], dg),
                                op0=Alu.mult, op1=Alu.add)
                        ins.then_inc(sem["v_res1"], 1)

                    def ffn_res2():
                        ve.wait_ge(sem["t_d1"], 48 * f + 48)
                        for dg in range(4, 8):
                            ins = ve.scalar_tensor_tensor(
                                out=gsl(x_buf[f % 2], dg), in0=ps_dn(dg),
                                scalar=1.0 / 127.0, in1=gsl(x_buf[f % 2], dg),
                                op0=Alu.mult, op1=Alu.add)
                        ins.then_inc(sem["v_res2"], 1)

                    def ffn_res():
                        ffn_res1()
                        ffn_res2()

                    if j == NCH:
                        for hg in range(HG):
                            h_act_op(hg)
                        ffn_res()
                        break

                    for g in range(G):
                        scan_group(g)
                        if f >= 0 and g >= 2:
                            h_act_op(g - 2)
                    if f >= 0:
                        for hg in range(6, 10):
                            h_act_op(hg)
                    outpre_g(G - 2)
                    outpre_g(G - 1)
                    if f >= 0:
                        for hg in range(10, HG):
                            h_act_op(hg)
                        ffn_res1()
                    if j >= 1 and j + 1 < NCH:
                        ve.wait_ge(*sa_xa(j + 1))
                        ve.wait_ge(sem["t_u"], 24 * j)
                        ve.tensor_mul(xn_buf[(j + 1) % 2][:, 0:4 * TC],
                                      x_buf[(j + 1) % 2][:, 0:4 * TC],
                                      x_buf[(j + 1) % 2][:, 0:4 * TC],
                                      ).then_inc(sem["v_sqa"], 1)
                    if f >= 0:
                        ffn_res2()
                    resid_cg(0)
                    resid_cg(1)
                    if j >= 1 and j + 1 < NCH:
                        ve.wait_ge(*sa_xb(j + 1))
                        ve.tensor_mul(xn_buf[(j + 1) % 2][:, 4 * TC:8 * TC],
                                      x_buf[(j + 1) % 2][:, 4 * TC:8 * TC],
                                      x_buf[(j + 1) % 2][:, 4 * TC:8 * TC],
                                      ).then_inc(sem["v_sqb"], 1)
                    resid_cg(2)
                    resid_cg(3)
                    if j + 1 < NCH:
                        ve.wait_ge(sem["t_stats"], j + 2)
                        ve.tensor_scalar_mul(m_row, ps_stats[0:1, 0:TC], 1.0 / C)
                        ve.tensor_scalar_mul(var_row, ps_stats[32:33, 0:TC], 1.0 / C)
                        ve.drain()
                        ve.tensor_mul(msq_row, m_row, m_row)
                        ve.drain()
                        ve.scalar_tensor_tensor(
                            out=var_row, in0=msq_row, scalar=-1.0, in1=var_row,
                            op0=Alu.mult, op1=Alu.add).then_inc(sem["v_var"], 1)
                    resid_cg(4)
                    if j + 1 < NCH:
                        ve.wait_ge(sem["s_srow"], j + 2)
                        ve.tensor_mul(ms_row, m_row, s_row).then_inc(sem["v_sv"], 1)
                    resid_cg(5)
                    if j + 1 < NCH:
                        ve.drain()
                        ve.engine_nop().then_inc(sem["v_mx"], 1)
                        ve.wait_ge(sem["s_mx"], j + 1)
                        ve.tensor_add(t1b, t1b, relu_u)
                        ve.drain()
                        ve.tensor_scalar_max(t1b, t1b, 0.0)
                        ve.drain()
                        ve.tensor_add(m_all2[q1], t1b, m_all2[q])
                        ve.drain()
                        ve.tensor_scalar_mul(neg_m2[q1], m_all2[q1], -1.0)
                        ve.tensor_sub(dp2[q1], m_all2[q], m_all2[q1]).then_inc(sem["v_m"], 1)
                        ve.wait_ge(sem["s_es"], j + 2)
                        ve.drain()
                        ve.tensor_mul(a02[q1], a_all, esb2[q1])
                        ve.tensor_mul(b02[q1], b_all, esb2[q1])
                    resid_cg(6)
                    if j + 1 < NCH:
                        ve.wait_ge(sem["s_msev"], j + 2)
                        if f >= 0:
                            ve.wait_ge(sem["t_u"], 24 * f + 24)
                        ve.drain()
                        for g in range(G):
                            ve.tensor_mul(xtmp[g], gsl(x_buf[(j + 1) % 2], g), sb_sb)
                        ve.drain()
                        for g in range(G - 1):
                            ve.tensor_sub(gsl(xn_buf[q1], g), xtmp[g], msb_sb)
                        ve.tensor_sub(gsl(xn_buf[q1], G - 1), xtmp[G - 1],
                                      msb_sb).then_inc(sem["v_xn"], 1)
                    resid_cg(7)
                    ve.drain()

            @blk.tensor
            def _(te):
                def stats_a(s):
                    te.wait_ge(sem["v_sqa"], s + 1)
                    if s >= 1:
                        te.wait_ge(sem["v_var"], s)
                    if s >= 2:
                        te.wait_ge(sem["v_res2"], s - 1)
                    for g in range(4):
                        te.matmul(ps_stats[0:1, 0:TC], onescol_f, gsl(x_buf[s % 2], g),
                                  start=(g == 0), stop=False, skip_group_check=True)
                    for g in range(4):
                        te.matmul(ps_stats[32:33, 0:TC], onescol_bf,
                                  gsl(xn_buf[s % 2], g),
                                  start=(g == 0), stop=False, skip_group_check=True)

                def stats_b(s):
                    te.wait_ge(sem["v_sqb"], s + 1)
                    for g in range(4, G):
                        te.matmul(ps_stats[0:1, 0:TC], onescol_f, gsl(x_buf[s % 2], g),
                                  start=False, stop=(g == G - 1), skip_group_check=True)
                    for g in range(4, G):
                        ins = te.matmul(ps_stats[32:33, 0:TC], onescol_bf,
                                        gsl(xn_buf[s % 2], g),
                                        start=False, stop=(g == G - 1),
                                        skip_group_check=True)
                    ins.then_inc(sem["t_stats"], 1)

                def bc_s(s):
                    te.wait_ge(sem["v_sv"], s + 1)
                    if s >= 1:
                        te.wait_ge(sem["s_msev"], s)
                    te.matmul(ps_bc, ones1[0:1, :], s_row,
                              start=True, stop=True).then_inc(sem["t_bcs"], 1)

                def bc_m(s):
                    te.wait_ge(sem["s_sev"], s + 1)
                    te.matmul(ps_bc, ones1[0:1, :], ms_row,
                              start=True, stop=True).then_inc(sem["t_bcm"], 1)

                for j in range(NCH + 1):
                    f = j - 1

                    def k_og(og):
                        kidx = 8 * j + og
                        if og == 0:
                            te.wait_ge(sem["v_xn"], j + 1)
                            if j == 0:
                                te.wait_ge(sem["g_wk"], 16)
                                te.wait_ge(sem["g_wr"], 16)
                        if kidx >= 1:
                            te.wait_ge(sem["s_ekd"], kidx)
                        for gk in range(G):
                            ins = te.matmul(
                                ps_k,
                                wk_bf[:, gk * 1024 + og * D:gk * 1024 + (og + 1) * D],
                                gsl(xn_buf[j % 2], gk),
                                start=(gk == 0), stop=(gk == G - 1))
                        ins.then_inc(sem["t_kog"], 1)

                    def r_og(og):
                        kidx = 8 * j + og
                        if og == 0 and j >= 1:
                            te.wait_ge(sem["v_res"], 8 * j)
                        if kidx >= 1:
                            te.wait_ge(sem["s_rev"], kidx)
                        for gk in range(G):
                            ins = te.matmul(
                                ps_r,
                                wr_bf[:, gk * 1024 + og * D:gk * 1024 + (og + 1) * D],
                                gsl(xn_buf[j % 2], gk),
                                start=(gk == 0), stop=(gk == G - 1))
                        ins.then_inc(sem["t_rog"], 1)

                    def proj_og(og):
                        k_og(og)
                        r_og(og)

                    def ffn_gu(hg):
                        ti = 24 * f + hg
                        te.wait_ge(*tile_done("wgt", ti))
                        if ti >= 1:
                            te.wait_ge(sem["s_sil"], ti)
                        for gc in range(G):
                            ins = te.matmul(
                                ps_gate, wgt[ti % 2][:, gc * D:(gc + 1) * D],
                                gsl(xn_buf[f % 2], gc),
                                start=(gc == 0), stop=(gc == G - 1))
                        ins.then_inc(sem["t_g"], 1)
                        te.wait_ge(*tile_done("wut", ti))
                        if ti >= 1:
                            te.wait_ge(sem["s_upev"], ti)
                        for gc in range(G):
                            ins = te.matmul(
                                ps_up, wut[ti % 2][:, gc * D:(gc + 1) * D],
                                gsl(xn_buf[f % 2], gc),
                                start=(gc == 0), stop=(gc == G - 1))
                        ins.then_inc(sem["t_u"], 1)

                    if j == 0:
                        stats_a(0)
                        stats_b(0)
                        bc_s(0)
                        bc_m(0)
                    if j < NCH and f < 0:
                        for og in range(G):
                            proj_og(og)
                    if f >= 0:
                        ffn_gu(0)
                        if j < NCH:
                            for og in range(G):
                                k_og(og)
                                r_og(og)
                                if og >= 2:
                                    ffn_gu(og - 1)
                        for hg in range(7 if j < NCH else 1, HG):
                            ffn_gu(hg)
                        for hg in range(HG):
                            if hg == 0:
                                te.wait_ge(sem["v_var"], min(f + 2, NCH))
                                te.wait_ge(sem["s_msev"], min(f + 2, NCH))
                                if f >= 1:
                                    te.wait_ge(sem["v_res2"], f)
                            te.wait_ge(sem["v_hact"], 24 * f + hg + 1)
                            ti48 = 48 * f + hg
                            te.wait_ge(*tile_done("wdt", ti48))
                            for dg in range(4):
                                ins = te.matmul(
                                    ps_dn(dg),
                                    wdt[ti48 % 2][:, dg * D:(dg + 1) * D],
                                    h_act[:, hg * TC:(hg + 1) * TC],
                                    start=(hg == 0), stop=(hg == HG - 1))
                            ins.then_inc(sem["t_d1"], 1)
                        te.wait_ge(sem["v_res1"], f + 1)
                        for hg in range(HG):
                            ti48 = 48 * f + 24 + hg
                            te.wait_ge(*tile_done("wdt", ti48))
                            for dg in range(4, 8):
                                ins = te.matmul(
                                    ps_dn(dg), wdt[ti48 % 2][:, dg * D:(dg + 1) * D],
                                    h_act[:, hg * TC:(hg + 1) * TC],
                                    start=(hg == 0), stop=(hg == HG - 1))
                            ins.then_inc(sem["t_d1"], 1)
                    def wo_cg(cg):
                        cgidx = 8 * j + cg
                        if cg == 0:
                            te.wait_ge(sem["v_out"], 8 * j + 8)
                            if j == 0:
                                te.wait_ge(sem["g_wo"], 16)
                        if cgidx >= 1:
                            te.wait_ge(sem["v_res"], cgidx)
                        for hg in range(G):
                            ins = te.matmul(
                                ps_wo(cg),
                                wo_bf[:, hg * 1024 + cg * D:hg * 1024 + (cg + 1) * D],
                                gsl(outpre, hg),
                                start=(hg == 0), stop=(hg == G - 1))
                        ins.then_inc(sem["t_wo"], 1)

                    if j < NCH:
                        wo_cg(0)
                        wo_cg(1)
                        if j + 1 < NCH:
                            stats_a(j + 1)
                        wo_cg(2)
                        wo_cg(3)
                        if j + 1 < NCH:
                            stats_b(j + 1)
                        wo_cg(4)
                        if j + 1 < NCH:
                            bc_s(j + 1)
                        wo_cg(5)
                        wo_cg(6)
                        if j + 1 < NCH:
                            bc_m(j + 1)
                        wo_cg(7)

        sA.close()
    return nc


def kernel(L0_attn_x, L0_attn_Wk, L0_attn_Wr, L0_attn_Wo,
           L0_attn_time_decay, L0_attn_time_first,
           L0_ffn_w_gate, L0_ffn_w_up, L0_ffn_w_down):
    x = np.asarray(L0_attn_x, dtype=np.float32)
    wkT = np.ascontiguousarray(np.asarray(L0_attn_Wk, dtype=np.float32).T)
    wrT = np.ascontiguousarray(np.asarray(L0_attn_Wr, dtype=np.float32).T)
    woT = np.ascontiguousarray(np.asarray(L0_attn_Wo, dtype=np.float32).T)
    td = np.ascontiguousarray(np.asarray(L0_attn_time_decay, dtype=np.float32).reshape(G, D).T)
    tf = np.ascontiguousarray(np.asarray(L0_attn_time_first, dtype=np.float32).reshape(G, D).T)
    wgT = np.ascontiguousarray(np.asarray(L0_ffn_w_gate, dtype=np.int32).T)
    wuT = np.ascontiguousarray(np.asarray(L0_ffn_w_up, dtype=np.int32).T)
    wdT = np.ascontiguousarray(np.asarray(L0_ffn_w_down, dtype=np.int32).T)

    nc = build_nc()
    in_maps = []
    for b in range(NCORE):
        in_maps.append({
            "x_cm": np.ascontiguousarray(x[b].T),
            "wkT": wkT, "wrT": wrT, "woT": woT,
            "td_in": td, "tf_in": tf,
            "wgT": wgT, "wuT": wuT, "wdT": wdT,
        })
    res = run_bass_kernel_spmd(nc, in_maps, core_ids=list(range(NCORE)))
    out = np.stack([np.ascontiguousarray(r["out_cm"].T) for r in res.results])
    return out


# version 30
# speedup vs baseline: 1.0268x; 1.0268x over previous
import numpy as np
import ml_dtypes
import concourse.bass as bass
import concourse.mybir as mybir
from concourse.bass_utils import run_bass_kernel_spmd
from contextlib import ExitStack

F32 = mybir.dt.float32
BF16 = mybir.dt.bfloat16
I32 = mybir.dt.int32
Act = mybir.ActivationFunctionType
Alu = mybir.AluOpType
AX = mybir.AxisListType.X

NCORE = 8
T = 4096
C = 1024
HID = 3072
G = 8
HG = 24
TC = 512
NCH = T // TC
D = 128
P_INIT = -1.0e30
BK = 512


def build_nc():
    nc = bass.Bass("TRN2", target_bir_lowering=False)

    x_hbm = nc.dram_tensor("x_cm", [C, T], F32, kind="ExternalInput").ap()
    wk_hbm = nc.dram_tensor("wkT", [C, C], F32, kind="ExternalInput").ap()
    wr_hbm = nc.dram_tensor("wrT", [C, C], F32, kind="ExternalInput").ap()
    wo_hbm = nc.dram_tensor("woT", [C, C], F32, kind="ExternalInput").ap()
    td_hbm = nc.dram_tensor("td_in", [D, G], F32, kind="ExternalInput").ap()
    tf_hbm = nc.dram_tensor("tf_in", [D, G], F32, kind="ExternalInput").ap()
    wg_hbm = nc.dram_tensor("wgT", [C, HID], I32, kind="ExternalInput").ap()
    wu_hbm = nc.dram_tensor("wuT", [C, HID], I32, kind="ExternalInput").ap()
    wd_hbm = nc.dram_tensor("wdT", [HID, C], I32, kind="ExternalInput").ap()
    out_hbm = nc.dram_tensor("out_cm", [C, T], F32, kind="ExternalOutput").ap()
    wg_bfd = nc.dram_tensor("wg_bfd", [HG, D, C], BF16, kind="Internal").ap()
    wu_bfd = nc.dram_tensor("wu_bfd", [HG, D, C], BF16, kind="Internal").ap()
    wd_bfd = nc.dram_tensor("wd_bfd", [HID, C], BF16, kind="Internal").ap()

    def hbm_chunk(ap_, j):
        return ap_[:, j * TC:(j + 1) * TC].rearrange("(g p) t -> p g t", p=D)

    def hbm_half(ap_, j, h):
        return ap_[h * 512:(h + 1) * 512,
                   j * TC:(j + 1) * TC].rearrange("(g p) t -> p g t", p=D)

    es = ExitStack()
    with es:
        sem = {}
        for name in [
            "sPio", "g_wk", "g_wr", "g_wo", "s_pexp", "v_pw", "s_pew",
            "sA_xa0", "sA_xa1", "sA_xb0", "sA_xb1", "v_sqa", "v_sqb",
            "t_stats", "v_var", "s_srow", "v_sv",
            "t_bcs", "s_sev", "t_bcm", "s_msev", "v_xn", "v_xna",
            "t_kog", "t_rog", "v_mx", "s_mx", "s_rev", "s_ekd", "v_m", "s_es",
            "s_act", "v_den", "s_rec", "v_out", "t_wo", "v_res",
            "g_wgt0", "g_wgt1", "g_wut0", "g_wut1", "g_wdt0", "g_wdt1",
            "g_fcw1", "g_fcw2", "g_fcu1", "g_fcu2", "g_fcd",
            "t_g", "t_u", "t_d1", "s_sil", "s_upev",
            "v_hact", "t_p1", "v_res1", "t_p2", "v_res2",
            "g_oa0", "g_oa1", "g_ob0", "g_ob1",
        ]:
            sem[name] = es.enter_context(nc.semaphore(name))

        def sa_xa(j):
            return sem[f"sA_xa{j % 2}"], 16 * (j // 2 + 1)

        def sa_xb(j):
            return sem[f"sA_xb{j % 2}"], 16 * (j // 2 + 1)

        def g_oa(j):
            return sem[f"g_oa{j % 2}"], 16 * (j // 2 + 1)

        def g_ob(j):
            return sem[f"g_ob{j % 2}"], 16 * (j // 2 + 1)

        def tile_done(fam, ti):
            return sem[f"g_{fam}{ti % 2}"], 16 * (ti // 2 + 1)

        sA = ExitStack()
        tdt = sA.enter_context(nc.sbuf_tensor("tdt", [D, G], F32)).ap()
        tft = sA.enter_context(nc.sbuf_tensor("tft", [D, G], F32)).ap()
        exptd = sA.enter_context(nc.sbuf_tensor("exptd", [D, G], F32)).ap()
        wv = sA.enter_context(nc.sbuf_tensor("wv", [D, G], F32)).ap()
        ewv = sA.enter_context(nc.sbuf_tensor("ewv", [D, G], F32)).ap()
        euv = sA.enter_context(nc.sbuf_tensor("euv", [D, G], F32)).ap()
        relu_u = sA.enter_context(nc.sbuf_tensor("relu_u", [D, G], F32)).ap()
        a_all = sA.enter_context(nc.sbuf_tensor("a_all", [D, G], F32)).ap()
        b_all = sA.enter_context(nc.sbuf_tensor("b_all", [D, G], F32)).ap()
        Mall = sA.enter_context(nc.sbuf_tensor("Mall", [D, G], F32)).ap()
        t1b = sA.enter_context(nc.sbuf_tensor("t1b", [D, G], F32)).ap()
        m_all2 = [sA.enter_context(nc.sbuf_tensor(f"m_all{i}", [D, G], F32)).ap() for i in range(2)]
        neg_m2 = [sA.enter_context(nc.sbuf_tensor(f"neg_m{i}", [D, G], F32)).ap() for i in range(2)]
        dp2 = [sA.enter_context(nc.sbuf_tensor(f"dp{i}", [D, G], F32)).ap() for i in range(2)]
        esb2 = [sA.enter_context(nc.sbuf_tensor(f"esb{i}", [D, G], F32)).ap() for i in range(2)]
        a02 = [sA.enter_context(nc.sbuf_tensor(f"a0{i}", [D, G], F32)).ap() for i in range(2)]
        b02 = [sA.enter_context(nc.sbuf_tensor(f"b0{i}", [D, G], F32)).ap() for i in range(2)]
        onescol_bf = sA.enter_context(nc.sbuf_tensor("onescol_bf", [D, 1], BF16)).ap()
        onescol_f = sA.enter_context(nc.sbuf_tensor("onescol_f", [D, 1], F32)).ap()
        eps_t = sA.enter_context(nc.sbuf_tensor("eps_t", [D, 1], F32)).ap()
        nc.const_aps.register(F32, 1e-5, eps_t)
        ones1 = sA.enter_context(nc.sbuf_tensor("ones1", [1, D], F32)).ap()
        ew_1 = sA.enter_context(nc.sbuf_tensor("ew_1", [D, TC], F32)).ap()
        ones_tc = sA.enter_context(nc.sbuf_tensor("ones_tc", [D, TC], F32)).ap()
        wk_bf = sA.enter_context(nc.sbuf_tensor("wk_bf", [D, G * 1024], BF16)).ap()
        wr_bf = sA.enter_context(nc.sbuf_tensor("wr_bf", [D, G * 1024], BF16)).ap()
        wo_bf = sA.enter_context(nc.sbuf_tensor("wo_bf", [D, G * 1024], BF16)).ap()
        x_buf = [sA.enter_context(nc.sbuf_tensor(f"x_buf{i}", [D, G * TC], F32)).ap()
                 for i in range(2)]
        xn_buf = [sA.enter_context(nc.sbuf_tensor(f"xn_buf{i}", [D, G * TC], BF16)).ap()
                  for i in range(2)]
        r_sb = sA.enter_context(nc.sbuf_tensor("r_sb", [D, G * TC], F32)).ap()
        outpre = sA.enter_context(nc.sbuf_tensor("outpre", [D, G * TC], BF16)).ap()
        rows = sA.enter_context(nc.sbuf_tensor("rows", [1, 3 * TC], F32)).ap()
        m_row = rows[0:1, 0:TC]
        msq_row = rows[0:1, TC:2 * TC]
        s_row = rows[0:1, 2 * TC:3 * TC]
        ms_row = m_row
        var_row = s_row
        sb_sb = sA.enter_context(nc.sbuf_tensor("sb_sb", [D, TC], F32)).ap()
        msb_sb = sA.enter_context(nc.sbuf_tensor("msb_sb", [D, TC], F32)).ap()
        ek_s = [sA.enter_context(nc.sbuf_tensor(f"ek{i}", [D, TC], F32)).ap() for i in range(3)]
        enr_s = [sA.enter_context(nc.sbuf_tensor(f"enr{i}", [D, TC], F32)).ap() for i in range(3)]
        zA_s = [sA.enter_context(nc.sbuf_tensor(f"zA{i}", [D, TC], F32)).ap() for i in range(2)]
        eu_1 = sA.enter_context(nc.sbuf_tensor("eu_1", [D, TC], F32)).ap()
        den_1 = sA.enter_context(nc.sbuf_tensor("den_1", [D, TC], F32)).ap()
        SA_e = [sA.enter_context(nc.sbuf_tensor(f"SA{i}", [D, TC + 4], F32)).ap() for i in range(2)]
        SB_e = [sA.enter_context(nc.sbuf_tensor(f"SB{i}", [D, TC + 4], F32)).ap() for i in range(2)]
        h_act = sA.enter_context(nc.sbuf_tensor("h_act", [D, HG * TC], BF16)).ap()
        silu_sb = [sA.enter_context(nc.sbuf_tensor(f"silu{i}", [D, TC], F32)).ap() for i in range(2)]
        up_sb = [sA.enter_context(nc.sbuf_tensor(f"up{i}", [D, TC], F32)).ap() for i in range(2)]
        wgt = [sA.enter_context(nc.sbuf_tensor(f"wgt{i}", [D, C], BF16)).ap() for i in range(2)]
        wut = [sA.enter_context(nc.sbuf_tensor(f"wut{i}", [D, C], BF16)).ap() for i in range(2)]
        wdt = [sA.enter_context(nc.sbuf_tensor(f"wdt{i}", [D, C], BF16)).ap() for i in range(2)]
        ps = sA.enter_context(nc.psum_tensor("ps", [D, 8 * BK], F32)).ap()

        def bank(i, w=BK):
            return ps[:, i * BK:i * BK + w]

        ps_stats = ps
        ps_bc = bank(1)
        ps_k = bank(4)
        ps_r = bank(5)

        def ps_wo(cg):
            return bank(5)

        ps_gate = bank(6)
        ps_up = bank(7)

        def ps_dn(dg):
            return bank(dg % 4)

        def gsl(ap_, g, w=TC):
            return ap_[:, g * w:(g + 1) * w]

        with nc.Block() as blk:

            @blk.sync
            def _(sy):
                sy.dma_start(out=tdt, in_=td_hbm).then_inc(sem["sPio"], 16)
                sy.dma_start(out=tft, in_=tf_hbm).then_inc(sem["sPio"], 16)
                for jj in range(2):
                    sy.dma_start(
                        out=x_buf[jj][:, 0:4 * TC].rearrange("p (g t) -> p g t", t=TC),
                        in_=hbm_half(x_hbm, jj, 0)).then_inc(sem[f"sA_xa{jj % 2}"], 16)
                    sy.dma_start(
                        out=x_buf[jj][:, 4 * TC:8 * TC].rearrange("p (g t) -> p g t", t=TC),
                        in_=hbm_half(x_hbm, jj, 1)).then_inc(sem[f"sA_xb{jj % 2}"], 16)
                for j in range(NCH):
                    for hg in range(HG):
                        ti = 24 * j + hg
                        if j == 0:
                            sy.wait_ge(sem["g_fcw1" if hg < 12 else "g_fcw2"], 16 * 12)
                        if ti >= 2:
                            sy.wait_ge(sem["t_g"], ti - 1)
                        sy.dma_start(out=wgt[ti % 2],
                                     in_=wg_bfd[hg]).then_inc(sem[f"g_wgt{ti % 2}"], 16)
                        if j == 0:
                            sy.wait_ge(sem["g_fcu1" if hg < 12 else "g_fcu2"], 16 * 12)
                        if ti >= 2:
                            sy.wait_ge(sem["t_u"], ti - 1)
                        sy.dma_start(out=wut[ti % 2],
                                     in_=wu_bfd[hg]).then_inc(sem[f"g_wut{ti % 2}"], 16)
                    for hg2 in range(2 * HG):
                        ti48 = 48 * j + hg2
                        if j == 0 and hg2 == 0:
                            sy.wait_ge(sem["g_fcd"], 16 * 4)
                        if ti48 >= 2:
                            sy.wait_ge(sem["t_d1"], ti48 - 1)
                        hgw = hg2 % HG
                        sy.dma_start(out=wdt[ti48 % 2],
                                     in_=wd_bfd[hgw * D:(hgw + 1) * D, :],
                                     ).then_inc(sem[f"g_wdt{ti48 % 2}"], 16)

            @blk.gpsimd
            def _(gp):
                gp.wait_ge(sem["sA_xa0"], 16)
                gp.dma_start(
                    out=wk_bf.rearrange("p (g n) -> p g n", n=1024),
                    in_=wk_hbm.rearrange("(g p) n -> p g n", p=D)).then_inc(sem["g_wk"], 16)
                gp.dma_start(
                    out=wr_bf.rearrange("p (g n) -> p g n", n=1024),
                    in_=wr_hbm.rearrange("(g p) n -> p g n", p=D)).then_inc(sem["g_wr"], 16)
                gp.dma_start(
                    out=wo_bf.rearrange("p (g n) -> p g n", n=1024),
                    in_=wo_hbm.rearrange("(g p) n -> p g n", p=D)).then_inc(sem["g_wo"], 16)
                fpieces = []
                for hg in range(HG):
                    fpieces.append((
                        wg_bfd[hg].rearrange("p (g h) -> p g h", h=D),
                        wg_hbm[:, hg * D:(hg + 1) * D].rearrange("(g p) h -> p g h", p=D),
                        "g_fcw1" if hg < 12 else "g_fcw2"))
                for hg in range(HG):
                    fpieces.append((
                        wu_bfd[hg].rearrange("p (g h) -> p g h", h=D),
                        wu_hbm[:, hg * D:(hg + 1) * D].rearrange("(g p) h -> p g h", p=D),
                        "g_fcu1" if hg < 12 else "g_fcu2"))
                for rr in range(4):
                    fpieces.append((wd_bfd[rr * 768:(rr + 1) * 768, :],
                                    wd_hbm[rr * 768:(rr + 1) * 768, :], "g_fcd"))
                for pc in range(len(fpieces)):
                    gp.dma_start(out=fpieces[pc][0],
                                 in_=fpieces[pc][1]).then_inc(sem[fpieces[pc][2]], 16)
                for fch in range(NCH):
                    gp.wait_ge(sem["v_res1"], fch + 1)
                    gp.dma_start(
                        out=hbm_half(out_hbm, fch, 0),
                        in_=x_buf[fch % 2][:, 0:4 * TC].rearrange(
                            "p (g t) -> p g t", t=TC),
                    ).then_inc(sem[f"g_oa{fch % 2}"], 16)
                    gp.wait_ge(sem["v_res2"], fch + 1)
                    gp.dma_start(
                        out=hbm_half(out_hbm, fch, 1),
                        in_=x_buf[fch % 2][:, 4 * TC:8 * TC].rearrange(
                            "p (g t) -> p g t", t=TC),
                    ).then_inc(sem[f"g_ob{fch % 2}"], 16)
                gp.wait_ge(sem["g_oa0"], 16 * (NCH // 2))
                gp.wait_ge(sem["g_oa1"], 16 * (NCH // 2))
                gp.wait_ge(sem["g_ob0"], 16 * (NCH // 2))
                gp.wait_ge(sem["g_ob1"], 16 * (NCH // 2))

            @blk.scalar
            def _(sc):
                sc.wait_ge(sem["sPio"], 32)
                sc.activation(out=exptd, in_=tdt, func=Act.Exp).then_inc(sem["s_pexp"], 1)
                sc.wait_ge(sem["v_pw"], 1)
                sc.activation(out=ewv, in_=wv, func=Act.Exp)
                sc.activation(out=euv, in_=tft,
                              func=Act.Exp).then_inc(sem["s_pew"], 1)
                sc.wait_ge(sem["v_m"], 1)
                sc.activation(out=esb2[0], in_=dp2[0], func=Act.Exp).then_inc(sem["s_es"], 1)
                sc.wait_ge(sem["v_var"], 1)
                sc.activation(out=msq_row, in_=var_row, func=Act.Ln, bias=1e-5)
                sc.drain()
                sc.activation(out=s_row, in_=msq_row, func=Act.Exp,
                              scale=-0.5).then_inc(sem["s_srow"], 1)
                sc.wait_ge(sem["t_bcs"], 1)
                sc.copy(out=sb_sb, in_=ps_bc).then_inc(sem["s_sev"], 1)
                sc.wait_ge(sem["t_bcm"], 1)
                sc.copy(out=msb_sb, in_=ps_bc).then_inc(sem["s_msev"], 1)

                for j in range(NCH + 1):
                    q = j % 2
                    f = j - 1

                    def silu_up(hg):
                        ti = 24 * f + hg
                        if ti >= 2:
                            sc.wait_ge(sem["v_hact"], ti - 1)
                        sc.wait_ge(sem["t_g"], ti + 1)
                        sc.activation(out=silu_sb[hg % 2], in_=ps_gate, func=Act.Silu,
                                      scale=1.0 / 127.0).then_inc(sem["s_sil"], 1)
                        sc.wait_ge(sem["t_u"], ti + 1)
                        sc.copy(out=up_sb[hg % 2], in_=ps_up).then_inc(sem["s_upev"], 1)

                    if j == NCH:
                        for hg in range(HG):
                            silu_up(hg)
                        break

                    for og in range(G):
                        idx = 8 * j + og
                        if f >= 0 and og >= 2:
                            silu_up(og - 2)
                        if idx >= 3:
                            sc.wait_ge(sem["v_out"], idx - 2)
                            sc.wait_ge(sem["v_den"], idx - 2)
                        sc.wait_ge(sem["t_kog"], idx + 1)
                        sc.activation(out=ek_s[idx % 3], in_=ps_k, func=Act.Exp,
                                      bias=neg_m2[q][:, og:og + 1]).then_inc(sem["s_ekd"], 1)
                        if j >= 1:
                            sc.wait_ge(sem["v_den"], 8 * (j - 1) + og + 1)
                        sc.wait_ge(sem["t_rog"], idx + 1)
                        sc.copy(out=gsl(r_sb, og), in_=ps_r).then_inc(sem["s_rev"], 1)
                        sc.drain()
                        sc.activation(out=enr_s[idx % 3], in_=gsl(r_sb, og), func=Act.Exp,
                                      scale=-1.0).then_inc(sem["s_act"], 1)
                        if og >= 1:
                            pidx = idx - 1
                            sc.wait_ge(sem["v_den"], pidx + 1)
                            sc.activation(out=eu_1, in_=den_1, func=Act.Ln)
                            sc.drain()
                            sc.activation(out=ek_s[pidx % 3], in_=eu_1, func=Act.Exp,
                                          scale=-1.0).then_inc(sem["s_rec"], 1)
                    if f >= 0:
                        silu_up(6)
                        silu_up(7)
                    pidx = 8 * j + 7
                    sc.wait_ge(sem["v_den"], pidx + 1)
                    sc.activation(out=eu_1, in_=den_1, func=Act.Ln)
                    sc.drain()
                    sc.activation(out=ek_s[pidx % 3], in_=eu_1, func=Act.Exp,
                                  scale=-1.0).then_inc(sem["s_rec"], 1)
                    if f >= 0:
                        for hg in range(8, 24):
                            silu_up(hg)
                    if j >= 1 and j + 1 < NCH:
                        sc.wait_ge(*g_oa(j - 1))
                        sc.dma_start(
                            out=x_buf[(j + 1) % 2][:, 0:4 * TC].rearrange(
                                "p (g t) -> p g t", t=TC),
                            in_=hbm_half(x_hbm, j + 1, 0),
                        ).then_inc(sem[f"sA_xa{(j + 1) % 2}"], 16)
                        sc.wait_ge(*g_ob(j - 1))
                        sc.dma_start(
                            out=x_buf[(j + 1) % 2][:, 4 * TC:8 * TC].rearrange(
                                "p (g t) -> p g t", t=TC),
                            in_=hbm_half(x_hbm, j + 1, 1),
                        ).then_inc(sem[f"sA_xb{(j + 1) % 2}"], 16)
                    if j + 1 < NCH:
                        sc.wait_ge(sem["v_var"], j + 2)
                        sc.activation(out=msq_row, in_=var_row, func=Act.Ln, bias=1e-5)
                        sc.drain()
                        sc.activation(out=s_row, in_=msq_row, func=Act.Exp,
                                      scale=-0.5).then_inc(sem["s_srow"], 1)
                    if j + 1 < NCH:
                        sc.wait_ge(sem["v_mx"], j + 1)
                        sc.activation(out=t1b, in_=Mall,
                                      func=Act.Ln).then_inc(sem["s_mx"], 1)
                    if j + 1 < NCH:
                        q1 = (j + 1) % 2
                        sc.wait_ge(sem["v_m"], j + 2)
                        sc.activation(out=esb2[q1], in_=dp2[q1],
                                      func=Act.Exp).then_inc(sem["s_es"], 1)
                        sc.wait_ge(sem["t_bcs"], j + 2)
                        sc.copy(out=sb_sb, in_=ps_bc).then_inc(sem["s_sev"], 1)
                        sc.wait_ge(sem["t_bcm"], j + 2)
                        sc.copy(out=msb_sb, in_=ps_bc).then_inc(sem["s_msev"], 1)

            @blk.vector
            def _(ve):
                ve.memset(onescol_bf, 1.0)
                ve.memset(onescol_f, 1.0)
                ve.memset(eps_t, 1e-5)
                ve.memset(ones1, 1.0)
                ve.memset(a_all, 0.0)
                ve.memset(b_all, 0.0)
                ve.memset(Mall, 0.0)
                ve.memset(m_all2[1], P_INIT)
                ve.memset(den_1, 1.0)
                ve.drain()
                ve.wait_ge(sem["s_pexp"], 1)
                ve.tensor_scalar_mul(wv, exptd, -1.0).then_inc(sem["v_pw"], 1)
                ve.drain()
                ve.memset(ones_tc, 1.0)
                ve.wait_ge(sem["sPio"], 32)
                ve.tensor_scalar_max(relu_u, tft, 0.0)
                ve.drain()
                ve.tensor_add(t1b, Mall, relu_u)
                ve.drain()
                ve.tensor_max(m_all2[0], t1b, m_all2[1])
                ve.drain()
                ve.tensor_scalar_mul(neg_m2[0], m_all2[0], -1.0)
                ve.tensor_sub(dp2[0], m_all2[1], m_all2[0]).then_inc(sem["v_m"], 1)
                ve.wait_ge(*sa_xa(0))
                ve.tensor_mul(xn_buf[0][:, 0:4 * TC], x_buf[0][:, 0:4 * TC],
                              x_buf[0][:, 0:4 * TC]).then_inc(sem["v_sqa"], 1)
                ve.wait_ge(*sa_xb(0))
                ve.tensor_mul(xn_buf[0][:, 4 * TC:8 * TC], x_buf[0][:, 4 * TC:8 * TC],
                              x_buf[0][:, 4 * TC:8 * TC]).then_inc(sem["v_sqb"], 1)
                ve.wait_ge(sem["s_es"], 1)
                ve.tensor_mul(a02[0], a_all, esb2[0])
                ve.tensor_mul(b02[0], b_all, esb2[0])
                ve.wait_ge(sem["t_stats"], 1)
                ve.tensor_scalar_mul(m_row, ps_stats[0:1, 0:TC], 1.0 / C)
                ve.tensor_scalar_mul(var_row, ps_stats[32:33, 0:TC], 1.0 / C)
                ve.drain()
                ve.tensor_mul(msq_row, m_row, m_row)
                ve.drain()
                ve.scalar_tensor_tensor(
                    out=var_row, in0=msq_row, scalar=-1.0, in1=var_row,
                    op0=Alu.mult, op1=Alu.add).then_inc(sem["v_var"], 1)
                ve.wait_ge(sem["s_srow"], 1)
                ve.tensor_mul(ms_row, m_row, s_row).then_inc(sem["v_sv"], 1)
                xtmp = [zA_s[0], zA_s[1], den_1, eu_1,
                        SA_e[0][:, 0:TC], SA_e[1][:, 0:TC],
                        SB_e[0][:, 0:TC], SB_e[1][:, 0:TC]]
                ve.wait_ge(sem["s_msev"], 1)
                for g in range(G):
                    ve.tensor_mul(xtmp[g], gsl(x_buf[0], g), sb_sb)
                ve.drain()
                for g in range(3):
                    ve.tensor_sub(gsl(xn_buf[0], g), xtmp[g], msb_sb)
                ve.tensor_sub(gsl(xn_buf[0], 3), xtmp[3],
                              msb_sb).then_inc(sem["v_xna"], 1)
                for g in range(4, G - 1):
                    ve.tensor_sub(gsl(xn_buf[0], g), xtmp[g], msb_sb)
                ve.tensor_sub(gsl(xn_buf[0], G - 1), xtmp[G - 1],
                              msb_sb).then_inc(sem["v_xn"], 1)
                ve.wait_ge(*sa_xa(1))
                ve.tensor_mul(xn_buf[1][:, 0:4 * TC], x_buf[1][:, 0:4 * TC],
                              x_buf[1][:, 0:4 * TC]).then_inc(sem["v_sqa"], 1)
                ve.wait_ge(*sa_xb(1))
                ve.tensor_mul(xn_buf[1][:, 4 * TC:8 * TC], x_buf[1][:, 4 * TC:8 * TC],
                              x_buf[1][:, 4 * TC:8 * TC]).then_inc(sem["v_sqb"], 1)

                for j in range(NCH + 1):
                    q = j % 2
                    q1 = (j + 1) % 2
                    f = j - 1

                    def outpre_g(g):
                        idx = 8 * j + g
                        pg = g % 2
                        if g == 0:
                            ve.wait_ge(sem["t_wo"], 8 * j)
                        ve.wait_ge(sem["s_rec"], idx + 1)
                        ve.tensor_mul(gsl(outpre, g), zA_s[pg],
                                      ek_s[idx % 3]).then_inc(sem["v_out"], 1)

                    def scan_group(g):
                        idx = 8 * j + g
                        pg = g % 2
                        ve.wait_ge(sem["s_act"], idx + 1)
                        if g >= 2:
                            outpre_g(g - 2)
                        ve.tensor_copy(SA_e[pg][:, 0:1], a02[q][:, g:g + 1])
                        ve.tensor_copy(SB_e[pg][:, 0:1], b02[q][:, g:g + 1])
                        if idx == 0:
                            ve.wait_ge(sem["s_pew"], 1)
                        ve.tensor_scalar_mul(ew_1, ones_tc, ewv[:, g:g + 1])
                        ve.drain()
                        ve.tensor_mul(zA_s[pg], ek_s[idx % 3], gsl(r_sb, g))
                        ve.reduce_max(Mall[:, g:g + 1], ek_s[idx % 3], axis=AX)
                        ve.drain()
                        ve.tensor_tensor_scan(
                            out=SA_e[pg][:, 1:TC + 1], data0=ew_1,
                            data1=zA_s[pg], initial=a02[q][:, g:g + 1],
                            op0=Alu.mult, op1=Alu.add)
                        ve.tensor_tensor_scan(
                            out=SB_e[pg][:, 1:TC + 1], data0=ew_1,
                            data1=ek_s[idx % 3], initial=b02[q][:, g:g + 1],
                            op0=Alu.mult, op1=Alu.add)
                        ve.drain()
                        ve.scalar_tensor_tensor(
                            out=zA_s[pg], in0=zA_s[pg], scalar=euv[:, g:g + 1],
                            in1=SA_e[pg][:, 0:TC], op0=Alu.mult, op1=Alu.add)
                        if idx >= 1:
                            ve.wait_ge(sem["s_rec"], idx)
                        ve.scalar_tensor_tensor(
                            out=den_1, in0=ek_s[idx % 3], scalar=euv[:, g:g + 1],
                            in1=SB_e[pg][:, 0:TC], op0=Alu.mult, op1=Alu.add)
                        ve.tensor_copy(a_all[:, g:g + 1], SA_e[pg][:, TC:TC + 1])
                        ve.tensor_copy(b_all[:, g:g + 1], SB_e[pg][:, TC:TC + 1])
                        ve.drain()
                        ve.scalar_tensor_tensor(
                            out=den_1, in0=enr_s[idx % 3], scalar=1.0,
                            in1=den_1, op0=Alu.add,
                            op1=Alu.mult).then_inc(sem["v_den"], 1)

                    def h_act_op(hg):
                        ti = 24 * f + hg
                        ve.wait_ge(sem["s_sil"], ti + 1)
                        ve.wait_ge(sem["s_upev"], ti + 1)
                        ve.scalar_tensor_tensor(
                            out=h_act[:, hg * TC:(hg + 1) * TC], in0=up_sb[hg % 2],
                            scalar=1.0 / 127.0, in1=silu_sb[hg % 2], op0=Alu.mult,
                            op1=Alu.mult).then_inc(sem["v_hact"], 1)

                    def resid_cg(cg):
                        ve.wait_ge(sem["t_wo"], 8 * j + cg + 1)
                        ve.tensor_add(gsl(x_buf[j % 2], cg),
                                      gsl(x_buf[j % 2], cg),
                                      ps_wo(cg)).then_inc(sem["v_res"], 1)

                    def ffn_res1():
                        ve.wait_ge(sem["t_d1"], 48 * f + 24)
                        for dg in range(4):
                            ins = ve.scalar_tensor_tensor(
                                out=gsl(x_buf[f % 2], dg), in0=ps_dn(dg),
                                scalar=1.0 / 127.0, in1=gsl(x_buf[f % 2], dg),
                                op0=Alu.mult, op1=Alu.add)
                        ins.then_inc(sem["v_res1"], 1)

                    def ffn_res2():
                        ve.wait_ge(sem["t_d1"], 48 * f + 48)
                        for dg in range(4, 8):
                            ins = ve.scalar_tensor_tensor(
                                out=gsl(x_buf[f % 2], dg), in0=ps_dn(dg),
                                scalar=1.0 / 127.0, in1=gsl(x_buf[f % 2], dg),
                                op0=Alu.mult, op1=Alu.add)
                        ins.then_inc(sem["v_res2"], 1)

                    def ffn_res():
                        ffn_res1()
                        ffn_res2()

                    if j == NCH:
                        for hg in range(HG):
                            h_act_op(hg)
                        ffn_res()
                        break

                    for g in range(G):
                        scan_group(g)
                        if f >= 0 and g >= 2:
                            h_act_op(g - 2)
                    outpre_g(G - 2)
                    outpre_g(G - 1)
                    if f >= 0:
                        for hg in range(6, HG):
                            h_act_op(hg)
                        ffn_res1()
                    if j >= 1 and j + 1 < NCH:
                        ve.wait_ge(*sa_xa(j + 1))
                        ve.wait_ge(sem["t_u"], 24 * j)
                        ve.tensor_mul(xn_buf[(j + 1) % 2][:, 0:4 * TC],
                                      x_buf[(j + 1) % 2][:, 0:4 * TC],
                                      x_buf[(j + 1) % 2][:, 0:4 * TC],
                                      ).then_inc(sem["v_sqa"], 1)
                    if f >= 0:
                        ffn_res2()
                    resid_cg(0)
                    resid_cg(1)
                    if j >= 1 and j + 1 < NCH:
                        ve.wait_ge(*sa_xb(j + 1))
                        ve.tensor_mul(xn_buf[(j + 1) % 2][:, 4 * TC:8 * TC],
                                      x_buf[(j + 1) % 2][:, 4 * TC:8 * TC],
                                      x_buf[(j + 1) % 2][:, 4 * TC:8 * TC],
                                      ).then_inc(sem["v_sqb"], 1)
                    resid_cg(2)
                    resid_cg(3)
                    if j + 1 < NCH:
                        ve.wait_ge(sem["t_stats"], j + 2)
                        ve.tensor_scalar_mul(m_row, ps_stats[0:1, 0:TC], 1.0 / C)
                        ve.tensor_scalar_mul(var_row, ps_stats[32:33, 0:TC], 1.0 / C)
                        ve.drain()
                        ve.tensor_mul(msq_row, m_row, m_row)
                        ve.drain()
                        ve.scalar_tensor_tensor(
                            out=var_row, in0=msq_row, scalar=-1.0, in1=var_row,
                            op0=Alu.mult, op1=Alu.add).then_inc(sem["v_var"], 1)
                    resid_cg(4)
                    if j + 1 < NCH:
                        ve.wait_ge(sem["s_srow"], j + 2)
                        ve.tensor_mul(ms_row, m_row, s_row).then_inc(sem["v_sv"], 1)
                    resid_cg(5)
                    if j + 1 < NCH:
                        ve.drain()
                        ve.engine_nop().then_inc(sem["v_mx"], 1)
                        ve.wait_ge(sem["s_mx"], j + 1)
                        ve.tensor_add(t1b, t1b, relu_u)
                        ve.drain()
                        ve.tensor_scalar_max(t1b, t1b, 0.0)
                        ve.drain()
                        ve.tensor_add(m_all2[q1], t1b, m_all2[q])
                        ve.drain()
                        ve.tensor_scalar_mul(neg_m2[q1], m_all2[q1], -1.0)
                        ve.tensor_sub(dp2[q1], m_all2[q], m_all2[q1]).then_inc(sem["v_m"], 1)
                        ve.wait_ge(sem["s_es"], j + 2)
                        ve.drain()
                        ve.tensor_mul(a02[q1], a_all, esb2[q1])
                        ve.tensor_mul(b02[q1], b_all, esb2[q1])
                    resid_cg(6)
                    if j + 1 < NCH:
                        ve.wait_ge(sem["s_msev"], j + 2)
                        if f >= 0:
                            ve.wait_ge(sem["t_u"], 24 * f + 24)
                        ve.drain()
                        for g in range(4):
                            ve.tensor_mul(xtmp[g], gsl(x_buf[(j + 1) % 2], g), sb_sb)
                        ve.drain()
                        for g in range(3):
                            ve.tensor_sub(gsl(xn_buf[q1], g), xtmp[g], msb_sb)
                        ve.tensor_sub(gsl(xn_buf[q1], 3), xtmp[3],
                                      msb_sb).then_inc(sem["v_xna"], 1)
                        for g in range(4, G):
                            ve.tensor_mul(xtmp[g], gsl(x_buf[(j + 1) % 2], g), sb_sb)
                        ve.drain()
                        for g in range(4, G - 1):
                            ve.tensor_sub(gsl(xn_buf[q1], g), xtmp[g], msb_sb)
                        ve.tensor_sub(gsl(xn_buf[q1], G - 1), xtmp[G - 1],
                                      msb_sb).then_inc(sem["v_xn"], 1)
                    resid_cg(7)
                    ve.drain()

            @blk.tensor
            def _(te):
                def stats_a(s):
                    te.wait_ge(sem["v_sqa"], s + 1)
                    if s >= 1:
                        te.wait_ge(sem["v_var"], s)
                    if s >= 2:
                        te.wait_ge(sem["v_res2"], s - 1)
                    for g in range(4):
                        te.matmul(ps_stats[0:1, 0:TC], onescol_f, gsl(x_buf[s % 2], g),
                                  start=(g == 0), stop=False, skip_group_check=True)
                    for g in range(4):
                        te.matmul(ps_stats[32:33, 0:TC], onescol_bf,
                                  gsl(xn_buf[s % 2], g),
                                  start=(g == 0), stop=False, skip_group_check=True)

                def stats_b(s):
                    te.wait_ge(sem["v_sqb"], s + 1)
                    for g in range(4, G):
                        te.matmul(ps_stats[0:1, 0:TC], onescol_f, gsl(x_buf[s % 2], g),
                                  start=False, stop=(g == G - 1), skip_group_check=True)
                    for g in range(4, G):
                        ins = te.matmul(ps_stats[32:33, 0:TC], onescol_bf,
                                        gsl(xn_buf[s % 2], g),
                                        start=False, stop=(g == G - 1),
                                        skip_group_check=True)
                    ins.then_inc(sem["t_stats"], 1)

                def bc_s(s):
                    te.wait_ge(sem["v_sv"], s + 1)
                    if s >= 1:
                        te.wait_ge(sem["s_msev"], s)
                    te.matmul(ps_bc, ones1[0:1, :], s_row,
                              start=True, stop=True).then_inc(sem["t_bcs"], 1)

                def bc_m(s):
                    te.wait_ge(sem["s_sev"], s + 1)
                    te.matmul(ps_bc, ones1[0:1, :], ms_row,
                              start=True, stop=True).then_inc(sem["t_bcm"], 1)

                for j in range(NCH + 1):
                    f = j - 1

                    def k_og(og):
                        kidx = 8 * j + og
                        if og == 0:
                            te.wait_ge(sem["v_xna"], j + 1)
                            if j == 0:
                                te.wait_ge(sem["g_wk"], 16)
                                te.wait_ge(sem["g_wr"], 16)
                        if kidx >= 1:
                            te.wait_ge(sem["s_ekd"], kidx)
                        for gk in range(G):
                            if og == 0 and gk == 4:
                                te.wait_ge(sem["v_xn"], j + 1)
                            ins = te.matmul(
                                ps_k,
                                wk_bf[:, gk * 1024 + og * D:gk * 1024 + (og + 1) * D],
                                gsl(xn_buf[j % 2], gk),
                                start=(gk == 0), stop=(gk == G - 1))
                        ins.then_inc(sem["t_kog"], 1)

                    def r_og(og):
                        kidx = 8 * j + og
                        if og == 0 and j >= 1:
                            te.wait_ge(sem["v_res"], 8 * j)
                        if kidx >= 1:
                            te.wait_ge(sem["s_rev"], kidx)
                        for gk in range(G):
                            ins = te.matmul(
                                ps_r,
                                wr_bf[:, gk * 1024 + og * D:gk * 1024 + (og + 1) * D],
                                gsl(xn_buf[j % 2], gk),
                                start=(gk == 0), stop=(gk == G - 1))
                        ins.then_inc(sem["t_rog"], 1)

                    def proj_og(og):
                        k_og(og)
                        r_og(og)

                    def ffn_gu(hg):
                        ti = 24 * f + hg
                        te.wait_ge(*tile_done("wgt", ti))
                        if ti >= 1:
                            te.wait_ge(sem["s_sil"], ti)
                        for gc in range(G):
                            ins = te.matmul(
                                ps_gate, wgt[ti % 2][:, gc * D:(gc + 1) * D],
                                gsl(xn_buf[f % 2], gc),
                                start=(gc == 0), stop=(gc == G - 1))
                        ins.then_inc(sem["t_g"], 1)
                        te.wait_ge(*tile_done("wut", ti))
                        if ti >= 1:
                            te.wait_ge(sem["s_upev"], ti)
                        for gc in range(G):
                            ins = te.matmul(
                                ps_up, wut[ti % 2][:, gc * D:(gc + 1) * D],
                                gsl(xn_buf[f % 2], gc),
                                start=(gc == 0), stop=(gc == G - 1))
                        ins.then_inc(sem["t_u"], 1)

                    if j == 0:
                        stats_a(0)
                        stats_b(0)
                        bc_s(0)
                        bc_m(0)
                    if j < NCH and f < 0:
                        for og in range(G):
                            proj_og(og)
                    if f >= 0:
                        ffn_gu(0)
                        if j < NCH:
                            for og in range(G):
                                k_og(og)
                                r_og(og)
                                if og >= 2:
                                    ffn_gu(og - 1)
                        for hg in range(7 if j < NCH else 1, HG):
                            ffn_gu(hg)
                        for hg in range(HG):
                            if hg == 0:
                                te.wait_ge(sem["v_var"], min(f + 2, NCH))
                                te.wait_ge(sem["s_msev"], min(f + 2, NCH))
                                if f >= 1:
                                    te.wait_ge(sem["v_res2"], f)
                            te.wait_ge(sem["v_hact"], 24 * f + hg + 1)
                            ti48 = 48 * f + hg
                            te.wait_ge(*tile_done("wdt", ti48))
                            for dg in range(4):
                                ins = te.matmul(
                                    ps_dn(dg),
                                    wdt[ti48 % 2][:, dg * D:(dg + 1) * D],
                                    h_act[:, hg * TC:(hg + 1) * TC],
                                    start=(hg == 0), stop=(hg == HG - 1))
                            ins.then_inc(sem["t_d1"], 1)
                        te.wait_ge(sem["v_res1"], f + 1)
                        for hg in range(HG):
                            ti48 = 48 * f + 24 + hg
                            te.wait_ge(*tile_done("wdt", ti48))
                            for dg in range(4, 8):
                                ins = te.matmul(
                                    ps_dn(dg), wdt[ti48 % 2][:, dg * D:(dg + 1) * D],
                                    h_act[:, hg * TC:(hg + 1) * TC],
                                    start=(hg == 0), stop=(hg == HG - 1))
                            ins.then_inc(sem["t_d1"], 1)
                    def wo_cg(cg):
                        cgidx = 8 * j + cg
                        if cg == 0:
                            te.wait_ge(sem["v_out"], 8 * j + 8)
                            if j == 0:
                                te.wait_ge(sem["g_wo"], 16)
                        if cgidx >= 1:
                            te.wait_ge(sem["v_res"], cgidx)
                        for hg in range(G):
                            ins = te.matmul(
                                ps_wo(cg),
                                wo_bf[:, hg * 1024 + cg * D:hg * 1024 + (cg + 1) * D],
                                gsl(outpre, hg),
                                start=(hg == 0), stop=(hg == G - 1))
                        ins.then_inc(sem["t_wo"], 1)

                    if j < NCH:
                        wo_cg(0)
                        wo_cg(1)
                        if j + 1 < NCH:
                            stats_a(j + 1)
                        wo_cg(2)
                        wo_cg(3)
                        if j + 1 < NCH:
                            stats_b(j + 1)
                        wo_cg(4)
                        if j + 1 < NCH:
                            bc_s(j + 1)
                        wo_cg(5)
                        wo_cg(6)
                        if j + 1 < NCH:
                            bc_m(j + 1)
                        wo_cg(7)

        sA.close()
    return nc


def kernel(L0_attn_x, L0_attn_Wk, L0_attn_Wr, L0_attn_Wo,
           L0_attn_time_decay, L0_attn_time_first,
           L0_ffn_w_gate, L0_ffn_w_up, L0_ffn_w_down):
    x = np.asarray(L0_attn_x, dtype=np.float32)
    wkT = np.ascontiguousarray(np.asarray(L0_attn_Wk, dtype=np.float32).T)
    wrT = np.ascontiguousarray(np.asarray(L0_attn_Wr, dtype=np.float32).T)
    woT = np.ascontiguousarray(np.asarray(L0_attn_Wo, dtype=np.float32).T)
    td = np.ascontiguousarray(np.asarray(L0_attn_time_decay, dtype=np.float32).reshape(G, D).T)
    tf = np.ascontiguousarray(np.asarray(L0_attn_time_first, dtype=np.float32).reshape(G, D).T)
    wgT = np.ascontiguousarray(np.asarray(L0_ffn_w_gate, dtype=np.int32).T)
    wuT = np.ascontiguousarray(np.asarray(L0_ffn_w_up, dtype=np.int32).T)
    wdT = np.ascontiguousarray(np.asarray(L0_ffn_w_down, dtype=np.int32).T)

    nc = build_nc()
    in_maps = []
    for b in range(NCORE):
        in_maps.append({
            "x_cm": np.ascontiguousarray(x[b].T),
            "wkT": wkT, "wrT": wrT, "woT": woT,
            "td_in": td, "tf_in": tf,
            "wgT": wgT, "wuT": wuT, "wdT": wdT,
        })
    res = run_bass_kernel_spmd(nc, in_maps, core_ids=list(range(NCORE)))
    out = np.stack([np.ascontiguousarray(r["out_cm"].T) for r in res.results])
    return out


# version 31
# speedup vs baseline: 1.0296x; 1.0027x over previous
import numpy as np
import ml_dtypes
import concourse.bass as bass
import concourse.mybir as mybir
from concourse.bass_utils import run_bass_kernel_spmd
from contextlib import ExitStack

F32 = mybir.dt.float32
BF16 = mybir.dt.bfloat16
I32 = mybir.dt.int32
Act = mybir.ActivationFunctionType
Alu = mybir.AluOpType
AX = mybir.AxisListType.X

NCORE = 8
T = 4096
C = 1024
HID = 3072
G = 8
HG = 24
TC = 512
NCH = T // TC
D = 128
P_INIT = -1.0e30
BK = 512


def build_nc():
    nc = bass.Bass("TRN2", target_bir_lowering=False)

    x_hbm = nc.dram_tensor("x_cm", [C, T], F32, kind="ExternalInput").ap()
    wk_hbm = nc.dram_tensor("wkT", [C, C], F32, kind="ExternalInput").ap()
    wr_hbm = nc.dram_tensor("wrT", [C, C], F32, kind="ExternalInput").ap()
    wo_hbm = nc.dram_tensor("woT", [C, C], F32, kind="ExternalInput").ap()
    td_hbm = nc.dram_tensor("td_in", [D, G], F32, kind="ExternalInput").ap()
    tf_hbm = nc.dram_tensor("tf_in", [D, G], F32, kind="ExternalInput").ap()
    wg_hbm = nc.dram_tensor("wgT", [C, HID], I32, kind="ExternalInput").ap()
    wu_hbm = nc.dram_tensor("wuT", [C, HID], I32, kind="ExternalInput").ap()
    wd_hbm = nc.dram_tensor("wdT", [HID, C], I32, kind="ExternalInput").ap()
    out_hbm = nc.dram_tensor("out_cm", [C, T], F32, kind="ExternalOutput").ap()
    wg_bfd = nc.dram_tensor("wg_bfd", [HG, D, C], BF16, kind="Internal").ap()
    wu_bfd = nc.dram_tensor("wu_bfd", [HG, D, C], BF16, kind="Internal").ap()
    wd_bfd = nc.dram_tensor("wd_bfd", [HID, C], BF16, kind="Internal").ap()

    def hbm_chunk(ap_, j):
        return ap_[:, j * TC:(j + 1) * TC].rearrange("(g p) t -> p g t", p=D)

    def hbm_half(ap_, j, h):
        return ap_[h * 512:(h + 1) * 512,
                   j * TC:(j + 1) * TC].rearrange("(g p) t -> p g t", p=D)

    es = ExitStack()
    with es:
        sem = {}
        for name in [
            "sPio", "g_wk", "g_wr", "g_wo", "s_pexp", "v_pw", "s_pew",
            "sA_xa0", "sA_xa1", "sA_xb0", "sA_xb1", "v_sqa", "v_sqb",
            "t_stats", "v_var", "s_srow", "v_sv",
            "t_bcs", "s_sev", "t_bcm", "s_msev", "v_xn", "v_xna",
            "t_kog", "t_rog", "v_mx", "s_mx", "s_rev", "s_ekd", "v_m", "s_es",
            "s_act", "v_den", "s_rec", "v_out", "t_wo", "v_res",
            "g_wgt0", "g_wgt1", "g_wut0", "g_wut1", "g_wdt0", "g_wdt1",
            "g_fcw1", "g_fcw2", "g_fcu1", "g_fcu2", "g_fcd",
            "t_g", "t_u", "t_d1", "s_sil", "s_upev",
            "v_hact", "t_p1", "v_res1", "t_p2", "v_res2",
            "g_oa0", "g_oa1", "g_ob0", "g_ob1",
        ]:
            sem[name] = es.enter_context(nc.semaphore(name))

        def sa_xa(j):
            return sem[f"sA_xa{j % 2}"], 16 * (j // 2 + 1)

        def sa_xb(j):
            return sem[f"sA_xb{j % 2}"], 16 * (j // 2 + 1)

        def g_oa(j):
            return sem[f"g_oa{j % 2}"], 16 * (j // 2 + 1)

        def g_ob(j):
            return sem[f"g_ob{j % 2}"], 16 * (j // 2 + 1)

        def tile_done(fam, ti):
            return sem[f"g_{fam}{ti % 2}"], 16 * (ti // 2 + 1)

        sA = ExitStack()
        tdt = sA.enter_context(nc.sbuf_tensor("tdt", [D, G], F32)).ap()
        tft = sA.enter_context(nc.sbuf_tensor("tft", [D, G], F32)).ap()
        exptd = sA.enter_context(nc.sbuf_tensor("exptd", [D, G], F32)).ap()
        wv = sA.enter_context(nc.sbuf_tensor("wv", [D, G], F32)).ap()
        ewv = sA.enter_context(nc.sbuf_tensor("ewv", [D, G], F32)).ap()
        euv = sA.enter_context(nc.sbuf_tensor("euv", [D, G], F32)).ap()
        relu_u = sA.enter_context(nc.sbuf_tensor("relu_u", [D, G], F32)).ap()
        a_all = sA.enter_context(nc.sbuf_tensor("a_all", [D, G], F32)).ap()
        b_all = sA.enter_context(nc.sbuf_tensor("b_all", [D, G], F32)).ap()
        Mall = sA.enter_context(nc.sbuf_tensor("Mall", [D, G], F32)).ap()
        t1b = sA.enter_context(nc.sbuf_tensor("t1b", [D, G], F32)).ap()
        m_all2 = [sA.enter_context(nc.sbuf_tensor(f"m_all{i}", [D, G], F32)).ap() for i in range(2)]
        neg_m2 = [sA.enter_context(nc.sbuf_tensor(f"neg_m{i}", [D, G], F32)).ap() for i in range(2)]
        dp2 = [sA.enter_context(nc.sbuf_tensor(f"dp{i}", [D, G], F32)).ap() for i in range(2)]
        esb2 = [sA.enter_context(nc.sbuf_tensor(f"esb{i}", [D, G], F32)).ap() for i in range(2)]
        a02 = [sA.enter_context(nc.sbuf_tensor(f"a0{i}", [D, G], F32)).ap() for i in range(2)]
        b02 = [sA.enter_context(nc.sbuf_tensor(f"b0{i}", [D, G], F32)).ap() for i in range(2)]
        onescol_bf = sA.enter_context(nc.sbuf_tensor("onescol_bf", [D, 1], BF16)).ap()
        onescol_f = sA.enter_context(nc.sbuf_tensor("onescol_f", [D, 1], F32)).ap()
        eps_t = sA.enter_context(nc.sbuf_tensor("eps_t", [D, 1], F32)).ap()
        nc.const_aps.register(F32, 1e-5, eps_t)
        ones1 = sA.enter_context(nc.sbuf_tensor("ones1", [1, D], F32)).ap()
        ew_1 = sA.enter_context(nc.sbuf_tensor("ew_1", [D, TC], F32)).ap()
        ones_tc = sA.enter_context(nc.sbuf_tensor("ones_tc", [D, TC], F32)).ap()
        wk_bf = sA.enter_context(nc.sbuf_tensor("wk_bf", [D, G * 1024], BF16)).ap()
        wr_bf = sA.enter_context(nc.sbuf_tensor("wr_bf", [D, G * 1024], BF16)).ap()
        wo_bf = sA.enter_context(nc.sbuf_tensor("wo_bf", [D, G * 1024], BF16)).ap()
        x_buf = [sA.enter_context(nc.sbuf_tensor(f"x_buf{i}", [D, G * TC], F32)).ap()
                 for i in range(2)]
        xn_buf = [sA.enter_context(nc.sbuf_tensor(f"xn_buf{i}", [D, G * TC], BF16)).ap()
                  for i in range(2)]
        r_sb = sA.enter_context(nc.sbuf_tensor("r_sb", [D, G * TC], F32)).ap()
        outpre = sA.enter_context(nc.sbuf_tensor("outpre", [D, G * TC], BF16)).ap()
        rows = sA.enter_context(nc.sbuf_tensor("rows", [1, 3 * TC], F32)).ap()
        m_row = rows[0:1, 0:TC]
        msq_row = rows[0:1, TC:2 * TC]
        s_row = rows[0:1, 2 * TC:3 * TC]
        ms_row = m_row
        var_row = s_row
        sb_sb = sA.enter_context(nc.sbuf_tensor("sb_sb", [D, TC], F32)).ap()
        msb_sb = sA.enter_context(nc.sbuf_tensor("msb_sb", [D, TC], F32)).ap()
        ek_s = [sA.enter_context(nc.sbuf_tensor(f"ek{i}", [D, TC], F32)).ap() for i in range(3)]
        enr_s = [sA.enter_context(nc.sbuf_tensor(f"enr{i}", [D, TC], F32)).ap() for i in range(3)]
        zA_s = [sA.enter_context(nc.sbuf_tensor(f"zA{i}", [D, TC], F32)).ap() for i in range(2)]
        eu_1 = sA.enter_context(nc.sbuf_tensor("eu_1", [D, TC], F32)).ap()
        den_1 = sA.enter_context(nc.sbuf_tensor("den_1", [D, TC], F32)).ap()
        SA_e = [sA.enter_context(nc.sbuf_tensor(f"SA{i}", [D, TC + 4], F32)).ap() for i in range(2)]
        SB_e = [sA.enter_context(nc.sbuf_tensor(f"SB{i}", [D, TC + 4], F32)).ap() for i in range(2)]
        h_act = sA.enter_context(nc.sbuf_tensor("h_act", [D, HG * TC], BF16)).ap()
        silu_sb = [sA.enter_context(nc.sbuf_tensor(f"silu{i}", [D, TC], F32)).ap() for i in range(2)]
        up_sb = [sA.enter_context(nc.sbuf_tensor(f"up{i}", [D, TC], F32)).ap() for i in range(2)]
        wgt = [sA.enter_context(nc.sbuf_tensor(f"wgt{i}", [D, C], BF16)).ap() for i in range(2)]
        wut = [sA.enter_context(nc.sbuf_tensor(f"wut{i}", [D, C], BF16)).ap() for i in range(2)]
        wdt = [sA.enter_context(nc.sbuf_tensor(f"wdt{i}", [D, C], BF16)).ap() for i in range(2)]
        ps = sA.enter_context(nc.psum_tensor("ps", [D, 8 * BK], F32)).ap()

        def bank(i, w=BK):
            return ps[:, i * BK:i * BK + w]

        ps_stats = ps
        ps_bc = bank(1)
        ps_k = bank(4)
        ps_r = bank(5)

        def ps_wo(cg):
            return bank(5)

        ps_gate = bank(6)
        ps_up = bank(7)

        def ps_dn(dg):
            return bank(dg % 4)

        def gsl(ap_, g, w=TC):
            return ap_[:, g * w:(g + 1) * w]

        with nc.Block() as blk:

            @blk.sync
            def _(sy):
                sy.dma_start(out=tdt, in_=td_hbm).then_inc(sem["sPio"], 16)
                sy.dma_start(out=tft, in_=tf_hbm).then_inc(sem["sPio"], 16)
                for jj in range(2):
                    sy.dma_start(
                        out=x_buf[jj][:, 0:4 * TC].rearrange("p (g t) -> p g t", t=TC),
                        in_=hbm_half(x_hbm, jj, 0)).then_inc(sem[f"sA_xa{jj % 2}"], 16)
                    sy.dma_start(
                        out=x_buf[jj][:, 4 * TC:8 * TC].rearrange("p (g t) -> p g t", t=TC),
                        in_=hbm_half(x_hbm, jj, 1)).then_inc(sem[f"sA_xb{jj % 2}"], 16)
                for j in range(NCH):
                    for hg in range(HG):
                        ti = 24 * j + hg
                        if j == 0:
                            sy.wait_ge(sem["g_fcw1" if hg < 12 else "g_fcw2"], 16 * 12)
                        if ti >= 2:
                            sy.wait_ge(sem["t_g"], ti - 1)
                        sy.dma_start(out=wgt[ti % 2],
                                     in_=wg_bfd[hg]).then_inc(sem[f"g_wgt{ti % 2}"], 16)
                        if j == 0:
                            sy.wait_ge(sem["g_fcu1" if hg < 12 else "g_fcu2"], 16 * 12)
                        if ti >= 2:
                            sy.wait_ge(sem["t_u"], ti - 1)
                        sy.dma_start(out=wut[ti % 2],
                                     in_=wu_bfd[hg]).then_inc(sem[f"g_wut{ti % 2}"], 16)
                    for hg2 in range(2 * HG):
                        ti48 = 48 * j + hg2
                        if j == 0 and hg2 == 0:
                            sy.wait_ge(sem["g_fcd"], 16 * 4)
                        if ti48 >= 2:
                            sy.wait_ge(sem["t_d1"], ti48 - 1)
                        hgw = hg2 % HG
                        sy.dma_start(out=wdt[ti48 % 2],
                                     in_=wd_bfd[hgw * D:(hgw + 1) * D, :],
                                     ).then_inc(sem[f"g_wdt{ti48 % 2}"], 16)

            @blk.gpsimd
            def _(gp):
                gp.wait_ge(sem["sA_xa0"], 16)
                gp.dma_start(
                    out=wk_bf.rearrange("p (g n) -> p g n", n=1024),
                    in_=wk_hbm.rearrange("(g p) n -> p g n", p=D)).then_inc(sem["g_wk"], 16)
                gp.dma_start(
                    out=wr_bf.rearrange("p (g n) -> p g n", n=1024),
                    in_=wr_hbm.rearrange("(g p) n -> p g n", p=D)).then_inc(sem["g_wr"], 16)
                gp.dma_start(
                    out=wo_bf.rearrange("p (g n) -> p g n", n=1024),
                    in_=wo_hbm.rearrange("(g p) n -> p g n", p=D)).then_inc(sem["g_wo"], 16)
                fpieces = []
                for hg in range(HG):
                    fpieces.append((
                        wg_bfd[hg].rearrange("p (g h) -> p g h", h=D),
                        wg_hbm[:, hg * D:(hg + 1) * D].rearrange("(g p) h -> p g h", p=D),
                        "g_fcw1" if hg < 12 else "g_fcw2"))
                for hg in range(HG):
                    fpieces.append((
                        wu_bfd[hg].rearrange("p (g h) -> p g h", h=D),
                        wu_hbm[:, hg * D:(hg + 1) * D].rearrange("(g p) h -> p g h", p=D),
                        "g_fcu1" if hg < 12 else "g_fcu2"))
                for rr in range(4):
                    fpieces.append((wd_bfd[rr * 768:(rr + 1) * 768, :],
                                    wd_hbm[rr * 768:(rr + 1) * 768, :], "g_fcd"))
                for pc in range(len(fpieces)):
                    gp.dma_start(out=fpieces[pc][0],
                                 in_=fpieces[pc][1]).then_inc(sem[fpieces[pc][2]], 16)
                for fch in range(NCH):
                    gp.wait_ge(sem["v_res1"], fch + 1)
                    gp.dma_start(
                        out=hbm_half(out_hbm, fch, 0),
                        in_=x_buf[fch % 2][:, 0:4 * TC].rearrange(
                            "p (g t) -> p g t", t=TC),
                    ).then_inc(sem[f"g_oa{fch % 2}"], 16)
                    gp.wait_ge(sem["v_res2"], fch + 1)
                    gp.dma_start(
                        out=hbm_half(out_hbm, fch, 1),
                        in_=x_buf[fch % 2][:, 4 * TC:8 * TC].rearrange(
                            "p (g t) -> p g t", t=TC),
                    ).then_inc(sem[f"g_ob{fch % 2}"], 16)
                gp.wait_ge(sem["g_oa0"], 16 * (NCH // 2))
                gp.wait_ge(sem["g_oa1"], 16 * (NCH // 2))
                gp.wait_ge(sem["g_ob0"], 16 * (NCH // 2))
                gp.wait_ge(sem["g_ob1"], 16 * (NCH // 2))

            @blk.scalar
            def _(sc):
                sc.wait_ge(sem["sPio"], 32)
                sc.activation(out=exptd, in_=tdt, func=Act.Exp).then_inc(sem["s_pexp"], 1)
                sc.wait_ge(sem["v_pw"], 1)
                sc.activation(out=ewv, in_=wv, func=Act.Exp)
                sc.activation(out=euv, in_=tft,
                              func=Act.Exp).then_inc(sem["s_pew"], 1)
                sc.wait_ge(sem["v_m"], 1)
                sc.activation(out=esb2[0], in_=dp2[0], func=Act.Exp).then_inc(sem["s_es"], 1)
                sc.wait_ge(sem["v_var"], 1)
                sc.activation(out=msq_row, in_=var_row, func=Act.Ln, bias=1e-5)
                sc.drain()
                sc.activation(out=s_row, in_=msq_row, func=Act.Exp,
                              scale=-0.5).then_inc(sem["s_srow"], 1)
                sc.wait_ge(sem["t_bcs"], 1)
                sc.copy(out=sb_sb, in_=ps_bc).then_inc(sem["s_sev"], 1)
                sc.wait_ge(sem["t_bcm"], 1)
                sc.copy(out=msb_sb, in_=ps_bc).then_inc(sem["s_msev"], 1)

                for j in range(NCH + 1):
                    q = j % 2
                    f = j - 1

                    def silu_up(hg):
                        ti = 24 * f + hg
                        if ti >= 2:
                            sc.wait_ge(sem["v_hact"], ti - 1)
                        sc.wait_ge(sem["t_g"], ti + 1)
                        sc.activation(out=silu_sb[hg % 2], in_=ps_gate, func=Act.Silu,
                                      scale=1.0 / 127.0).then_inc(sem["s_sil"], 1)
                        sc.wait_ge(sem["t_u"], ti + 1)
                        sc.copy(out=up_sb[hg % 2], in_=ps_up).then_inc(sem["s_upev"], 1)

                    if j == NCH:
                        for hg in range(HG):
                            silu_up(hg)
                        break

                    for og in range(G):
                        idx = 8 * j + og
                        if f >= 0 and og >= 2:
                            silu_up(og - 2)
                        if idx >= 3:
                            sc.wait_ge(sem["v_out"], idx - 2)
                            sc.wait_ge(sem["v_den"], idx - 2)
                        sc.wait_ge(sem["t_kog"], idx + 1)
                        sc.activation(out=ek_s[idx % 3], in_=ps_k, func=Act.Exp,
                                      bias=neg_m2[q][:, og:og + 1]).then_inc(sem["s_ekd"], 1)
                        if j >= 1:
                            sc.wait_ge(sem["v_den"], 8 * (j - 1) + og + 1)
                        sc.wait_ge(sem["t_rog"], idx + 1)
                        sc.copy(out=gsl(r_sb, og), in_=ps_r).then_inc(sem["s_rev"], 1)
                        sc.drain()
                        sc.activation(out=enr_s[idx % 3], in_=gsl(r_sb, og), func=Act.Exp,
                                      scale=-1.0).then_inc(sem["s_act"], 1)
                        if og >= 1:
                            pidx = idx - 1
                            sc.wait_ge(sem["v_den"], pidx + 1)
                            sc.activation(out=eu_1, in_=den_1, func=Act.Ln)
                            sc.drain()
                            sc.activation(out=ek_s[pidx % 3], in_=eu_1, func=Act.Exp,
                                          scale=-1.0).then_inc(sem["s_rec"], 1)
                    pidx = 8 * j + 7
                    sc.wait_ge(sem["v_den"], pidx + 1)
                    sc.activation(out=eu_1, in_=den_1, func=Act.Ln)
                    sc.drain()
                    sc.activation(out=ek_s[pidx % 3], in_=eu_1, func=Act.Exp,
                                  scale=-1.0).then_inc(sem["s_rec"], 1)
                    if f >= 0:
                        for hg in range(6, 24):
                            silu_up(hg)
                    if j >= 1 and j + 1 < NCH:
                        sc.wait_ge(*g_oa(j - 1))
                        sc.dma_start(
                            out=x_buf[(j + 1) % 2][:, 0:4 * TC].rearrange(
                                "p (g t) -> p g t", t=TC),
                            in_=hbm_half(x_hbm, j + 1, 0),
                        ).then_inc(sem[f"sA_xa{(j + 1) % 2}"], 16)
                        sc.wait_ge(*g_ob(j - 1))
                        sc.dma_start(
                            out=x_buf[(j + 1) % 2][:, 4 * TC:8 * TC].rearrange(
                                "p (g t) -> p g t", t=TC),
                            in_=hbm_half(x_hbm, j + 1, 1),
                        ).then_inc(sem[f"sA_xb{(j + 1) % 2}"], 16)
                    if j + 1 < NCH:
                        sc.wait_ge(sem["v_var"], j + 2)
                        sc.activation(out=msq_row, in_=var_row, func=Act.Ln, bias=1e-5)
                        sc.drain()
                        sc.activation(out=s_row, in_=msq_row, func=Act.Exp,
                                      scale=-0.5).then_inc(sem["s_srow"], 1)
                    if j + 1 < NCH:
                        sc.wait_ge(sem["v_mx"], j + 1)
                        sc.activation(out=t1b, in_=Mall,
                                      func=Act.Ln).then_inc(sem["s_mx"], 1)
                    if j + 1 < NCH:
                        q1 = (j + 1) % 2
                        sc.wait_ge(sem["v_m"], j + 2)
                        sc.activation(out=esb2[q1], in_=dp2[q1],
                                      func=Act.Exp).then_inc(sem["s_es"], 1)
                        sc.wait_ge(sem["t_bcs"], j + 2)
                        sc.copy(out=sb_sb, in_=ps_bc).then_inc(sem["s_sev"], 1)
                        sc.wait_ge(sem["t_bcm"], j + 2)
                        sc.copy(out=msb_sb, in_=ps_bc).then_inc(sem["s_msev"], 1)

            @blk.vector
            def _(ve):
                ve.memset(onescol_bf, 1.0)
                ve.memset(onescol_f, 1.0)
                ve.memset(eps_t, 1e-5)
                ve.memset(ones1, 1.0)
                ve.memset(a_all, 0.0)
                ve.memset(b_all, 0.0)
                ve.memset(Mall, 0.0)
                ve.memset(m_all2[1], P_INIT)
                ve.memset(den_1, 1.0)
                ve.drain()
                ve.wait_ge(sem["s_pexp"], 1)
                ve.tensor_scalar_mul(wv, exptd, -1.0).then_inc(sem["v_pw"], 1)
                ve.drain()
                ve.memset(ones_tc, 1.0)
                ve.wait_ge(sem["sPio"], 32)
                ve.tensor_scalar_max(relu_u, tft, 0.0)
                ve.drain()
                ve.tensor_add(t1b, Mall, relu_u)
                ve.drain()
                ve.tensor_max(m_all2[0], t1b, m_all2[1])
                ve.drain()
                ve.tensor_scalar_mul(neg_m2[0], m_all2[0], -1.0)
                ve.tensor_sub(dp2[0], m_all2[1], m_all2[0]).then_inc(sem["v_m"], 1)
                ve.wait_ge(*sa_xa(0))
                ve.tensor_mul(xn_buf[0][:, 0:4 * TC], x_buf[0][:, 0:4 * TC],
                              x_buf[0][:, 0:4 * TC]).then_inc(sem["v_sqa"], 1)
                ve.wait_ge(*sa_xb(0))
                ve.tensor_mul(xn_buf[0][:, 4 * TC:8 * TC], x_buf[0][:, 4 * TC:8 * TC],
                              x_buf[0][:, 4 * TC:8 * TC]).then_inc(sem["v_sqb"], 1)
                ve.wait_ge(sem["s_es"], 1)
                ve.tensor_mul(a02[0], a_all, esb2[0])
                ve.tensor_mul(b02[0], b_all, esb2[0])
                ve.wait_ge(sem["t_stats"], 1)
                ve.tensor_scalar_mul(m_row, ps_stats[0:1, 0:TC], 1.0 / C)
                ve.tensor_scalar_mul(var_row, ps_stats[32:33, 0:TC], 1.0 / C)
                ve.drain()
                ve.tensor_mul(msq_row, m_row, m_row)
                ve.drain()
                ve.scalar_tensor_tensor(
                    out=var_row, in0=msq_row, scalar=-1.0, in1=var_row,
                    op0=Alu.mult, op1=Alu.add).then_inc(sem["v_var"], 1)
                ve.wait_ge(sem["s_srow"], 1)
                ve.tensor_mul(ms_row, m_row, s_row).then_inc(sem["v_sv"], 1)
                xtmp = [zA_s[0], zA_s[1], den_1, eu_1,
                        SA_e[0][:, 0:TC], SA_e[1][:, 0:TC],
                        SB_e[0][:, 0:TC], SB_e[1][:, 0:TC]]
                ve.wait_ge(sem["s_msev"], 1)
                for g in range(G):
                    ve.tensor_mul(xtmp[g], gsl(x_buf[0], g), sb_sb)
                ve.drain()
                for g in range(3):
                    ve.tensor_sub(gsl(xn_buf[0], g), xtmp[g], msb_sb)
                ve.tensor_sub(gsl(xn_buf[0], 3), xtmp[3],
                              msb_sb).then_inc(sem["v_xna"], 1)
                for g in range(4, G - 1):
                    ve.tensor_sub(gsl(xn_buf[0], g), xtmp[g], msb_sb)
                ve.tensor_sub(gsl(xn_buf[0], G - 1), xtmp[G - 1],
                              msb_sb).then_inc(sem["v_xn"], 1)
                ve.wait_ge(*sa_xa(1))
                ve.tensor_mul(xn_buf[1][:, 0:4 * TC], x_buf[1][:, 0:4 * TC],
                              x_buf[1][:, 0:4 * TC]).then_inc(sem["v_sqa"], 1)
                ve.wait_ge(*sa_xb(1))
                ve.tensor_mul(xn_buf[1][:, 4 * TC:8 * TC], x_buf[1][:, 4 * TC:8 * TC],
                              x_buf[1][:, 4 * TC:8 * TC]).then_inc(sem["v_sqb"], 1)

                for j in range(NCH + 1):
                    q = j % 2
                    q1 = (j + 1) % 2
                    f = j - 1

                    def outpre_g(g):
                        idx = 8 * j + g
                        pg = g % 2
                        if g == 0:
                            ve.wait_ge(sem["t_wo"], 8 * j)
                        ve.wait_ge(sem["s_rec"], idx + 1)
                        ve.tensor_mul(gsl(outpre, g), zA_s[pg],
                                      ek_s[idx % 3]).then_inc(sem["v_out"], 1)

                    def scan_group(g):
                        idx = 8 * j + g
                        pg = g % 2
                        ve.wait_ge(sem["s_act"], idx + 1)
                        if g >= 2:
                            outpre_g(g - 2)
                        ve.tensor_copy(SA_e[pg][:, 0:1], a02[q][:, g:g + 1])
                        ve.tensor_copy(SB_e[pg][:, 0:1], b02[q][:, g:g + 1])
                        if idx == 0:
                            ve.wait_ge(sem["s_pew"], 1)
                        ve.tensor_scalar_mul(ew_1, ones_tc, ewv[:, g:g + 1])
                        ve.drain()
                        ve.tensor_mul(zA_s[pg], ek_s[idx % 3], gsl(r_sb, g))
                        ve.reduce_max(Mall[:, g:g + 1], ek_s[idx % 3], axis=AX)
                        ve.drain()
                        ve.tensor_tensor_scan(
                            out=SA_e[pg][:, 1:TC + 1], data0=ew_1,
                            data1=zA_s[pg], initial=a02[q][:, g:g + 1],
                            op0=Alu.mult, op1=Alu.add)
                        ve.tensor_tensor_scan(
                            out=SB_e[pg][:, 1:TC + 1], data0=ew_1,
                            data1=ek_s[idx % 3], initial=b02[q][:, g:g + 1],
                            op0=Alu.mult, op1=Alu.add)
                        ve.drain()
                        ve.scalar_tensor_tensor(
                            out=zA_s[pg], in0=zA_s[pg], scalar=euv[:, g:g + 1],
                            in1=SA_e[pg][:, 0:TC], op0=Alu.mult, op1=Alu.add)
                        if idx >= 1:
                            ve.wait_ge(sem["s_rec"], idx)
                        ve.scalar_tensor_tensor(
                            out=den_1, in0=ek_s[idx % 3], scalar=euv[:, g:g + 1],
                            in1=SB_e[pg][:, 0:TC], op0=Alu.mult, op1=Alu.add)
                        ve.tensor_copy(a_all[:, g:g + 1], SA_e[pg][:, TC:TC + 1])
                        ve.tensor_copy(b_all[:, g:g + 1], SB_e[pg][:, TC:TC + 1])
                        ve.drain()
                        ve.scalar_tensor_tensor(
                            out=den_1, in0=enr_s[idx % 3], scalar=1.0,
                            in1=den_1, op0=Alu.add,
                            op1=Alu.mult).then_inc(sem["v_den"], 1)

                    def h_act_op(hg):
                        ti = 24 * f + hg
                        ve.wait_ge(sem["s_sil"], ti + 1)
                        ve.wait_ge(sem["s_upev"], ti + 1)
                        ve.scalar_tensor_tensor(
                            out=h_act[:, hg * TC:(hg + 1) * TC], in0=up_sb[hg % 2],
                            scalar=1.0 / 127.0, in1=silu_sb[hg % 2], op0=Alu.mult,
                            op1=Alu.mult).then_inc(sem["v_hact"], 1)

                    def resid_cg(cg):
                        ve.wait_ge(sem["t_wo"], 8 * j + cg + 1)
                        ve.tensor_add(gsl(x_buf[j % 2], cg),
                                      gsl(x_buf[j % 2], cg),
                                      ps_wo(cg)).then_inc(sem["v_res"], 1)

                    def ffn_res1():
                        ve.wait_ge(sem["t_d1"], 48 * f + 24)
                        for dg in range(4):
                            ins = ve.scalar_tensor_tensor(
                                out=gsl(x_buf[f % 2], dg), in0=ps_dn(dg),
                                scalar=1.0 / 127.0, in1=gsl(x_buf[f % 2], dg),
                                op0=Alu.mult, op1=Alu.add)
                        ins.then_inc(sem["v_res1"], 1)

                    def ffn_res2():
                        ve.wait_ge(sem["t_d1"], 48 * f + 48)
                        for dg in range(4, 8):
                            ins = ve.scalar_tensor_tensor(
                                out=gsl(x_buf[f % 2], dg), in0=ps_dn(dg),
                                scalar=1.0 / 127.0, in1=gsl(x_buf[f % 2], dg),
                                op0=Alu.mult, op1=Alu.add)
                        ins.then_inc(sem["v_res2"], 1)

                    def ffn_res():
                        ffn_res1()
                        ffn_res2()

                    if j == NCH:
                        for hg in range(HG):
                            h_act_op(hg)
                        ffn_res()
                        break

                    for g in range(G):
                        scan_group(g)
                        if f >= 0 and g >= 2:
                            h_act_op(g - 2)
                    outpre_g(G - 2)
                    outpre_g(G - 1)
                    if f >= 0:
                        for hg in range(6, HG):
                            h_act_op(hg)
                        ffn_res1()
                    if j >= 1 and j + 1 < NCH:
                        ve.wait_ge(*sa_xa(j + 1))
                        ve.wait_ge(sem["t_u"], 24 * j)
                        ve.tensor_mul(xn_buf[(j + 1) % 2][:, 0:4 * TC],
                                      x_buf[(j + 1) % 2][:, 0:4 * TC],
                                      x_buf[(j + 1) % 2][:, 0:4 * TC],
                                      ).then_inc(sem["v_sqa"], 1)
                    if f >= 0:
                        ffn_res2()
                    resid_cg(0)
                    resid_cg(1)
                    if j >= 1 and j + 1 < NCH:
                        ve.wait_ge(*sa_xb(j + 1))
                        ve.tensor_mul(xn_buf[(j + 1) % 2][:, 4 * TC:8 * TC],
                                      x_buf[(j + 1) % 2][:, 4 * TC:8 * TC],
                                      x_buf[(j + 1) % 2][:, 4 * TC:8 * TC],
                                      ).then_inc(sem["v_sqb"], 1)
                    resid_cg(2)
                    resid_cg(3)
                    if j + 1 < NCH:
                        ve.wait_ge(sem["t_stats"], j + 2)
                        ve.tensor_scalar_mul(m_row, ps_stats[0:1, 0:TC], 1.0 / C)
                        ve.tensor_scalar_mul(var_row, ps_stats[32:33, 0:TC], 1.0 / C)
                        ve.drain()
                        ve.tensor_mul(msq_row, m_row, m_row)
                        ve.drain()
                        ve.scalar_tensor_tensor(
                            out=var_row, in0=msq_row, scalar=-1.0, in1=var_row,
                            op0=Alu.mult, op1=Alu.add).then_inc(sem["v_var"], 1)
                    resid_cg(4)
                    if j + 1 < NCH:
                        ve.wait_ge(sem["s_srow"], j + 2)
                        ve.tensor_mul(ms_row, m_row, s_row).then_inc(sem["v_sv"], 1)
                    resid_cg(5)
                    if j + 1 < NCH:
                        ve.drain()
                        ve.engine_nop().then_inc(sem["v_mx"], 1)
                        ve.wait_ge(sem["s_mx"], j + 1)
                        ve.tensor_add(t1b, t1b, relu_u)
                        ve.drain()
                        ve.tensor_scalar_max(t1b, t1b, 0.0)
                        ve.drain()
                        ve.tensor_add(m_all2[q1], t1b, m_all2[q])
                        ve.drain()
                        ve.tensor_scalar_mul(neg_m2[q1], m_all2[q1], -1.0)
                        ve.tensor_sub(dp2[q1], m_all2[q], m_all2[q1]).then_inc(sem["v_m"], 1)
                        ve.wait_ge(sem["s_es"], j + 2)
                        ve.drain()
                        ve.tensor_mul(a02[q1], a_all, esb2[q1])
                        ve.tensor_mul(b02[q1], b_all, esb2[q1])
                    resid_cg(6)
                    if j + 1 < NCH:
                        ve.wait_ge(sem["s_msev"], j + 2)
                        if f >= 0:
                            ve.wait_ge(sem["t_u"], 24 * f + 24)
                        ve.drain()
                        for g in range(4):
                            ve.tensor_mul(xtmp[g], gsl(x_buf[(j + 1) % 2], g), sb_sb)
                        ve.drain()
                        for g in range(3):
                            ve.tensor_sub(gsl(xn_buf[q1], g), xtmp[g], msb_sb)
                        ve.tensor_sub(gsl(xn_buf[q1], 3), xtmp[3],
                                      msb_sb).then_inc(sem["v_xna"], 1)
                        for g in range(4, G):
                            ve.tensor_mul(xtmp[g], gsl(x_buf[(j + 1) % 2], g), sb_sb)
                        ve.drain()
                        for g in range(4, G - 1):
                            ve.tensor_sub(gsl(xn_buf[q1], g), xtmp[g], msb_sb)
                        ve.tensor_sub(gsl(xn_buf[q1], G - 1), xtmp[G - 1],
                                      msb_sb).then_inc(sem["v_xn"], 1)
                    resid_cg(7)
                    ve.drain()

            @blk.tensor
            def _(te):
                def stats_a(s):
                    te.wait_ge(sem["v_sqa"], s + 1)
                    if s >= 1:
                        te.wait_ge(sem["v_var"], s)
                    if s >= 2:
                        te.wait_ge(sem["v_res2"], s - 1)
                    for g in range(4):
                        te.matmul(ps_stats[0:1, 0:TC], onescol_f, gsl(x_buf[s % 2], g),
                                  start=(g == 0), stop=False, skip_group_check=True)
                    for g in range(4):
                        te.matmul(ps_stats[32:33, 0:TC], onescol_bf,
                                  gsl(xn_buf[s % 2], g),
                                  start=(g == 0), stop=False, skip_group_check=True)

                def stats_b(s):
                    te.wait_ge(sem["v_sqb"], s + 1)
                    for g in range(4, G):
                        te.matmul(ps_stats[0:1, 0:TC], onescol_f, gsl(x_buf[s % 2], g),
                                  start=False, stop=(g == G - 1), skip_group_check=True)
                    for g in range(4, G):
                        ins = te.matmul(ps_stats[32:33, 0:TC], onescol_bf,
                                        gsl(xn_buf[s % 2], g),
                                        start=False, stop=(g == G - 1),
                                        skip_group_check=True)
                    ins.then_inc(sem["t_stats"], 1)

                def bc_s(s):
                    te.wait_ge(sem["v_sv"], s + 1)
                    if s >= 1:
                        te.wait_ge(sem["s_msev"], s)
                    te.matmul(ps_bc, ones1[0:1, :], s_row,
                              start=True, stop=True).then_inc(sem["t_bcs"], 1)

                def bc_m(s):
                    te.wait_ge(sem["s_sev"], s + 1)
                    te.matmul(ps_bc, ones1[0:1, :], ms_row,
                              start=True, stop=True).then_inc(sem["t_bcm"], 1)

                for j in range(NCH + 1):
                    f = j - 1

                    def k_og(og):
                        kidx = 8 * j + og
                        if og == 0:
                            te.wait_ge(sem["v_xna"], j + 1)
                            if j == 0:
                                te.wait_ge(sem["g_wk"], 16)
                                te.wait_ge(sem["g_wr"], 16)
                        if kidx >= 1:
                            te.wait_ge(sem["s_ekd"], kidx)
                        for gk in range(G):
                            if og == 0 and gk == 4:
                                te.wait_ge(sem["v_xn"], j + 1)
                            ins = te.matmul(
                                ps_k,
                                wk_bf[:, gk * 1024 + og * D:gk * 1024 + (og + 1) * D],
                                gsl(xn_buf[j % 2], gk),
                                start=(gk == 0), stop=(gk == G - 1))
                        ins.then_inc(sem["t_kog"], 1)

                    def r_og(og):
                        kidx = 8 * j + og
                        if og == 0 and j >= 1:
                            te.wait_ge(sem["v_res"], 8 * j)
                        if kidx >= 1:
                            te.wait_ge(sem["s_rev"], kidx)
                        for gk in range(G):
                            ins = te.matmul(
                                ps_r,
                                wr_bf[:, gk * 1024 + og * D:gk * 1024 + (og + 1) * D],
                                gsl(xn_buf[j % 2], gk),
                                start=(gk == 0), stop=(gk == G - 1))
                        ins.then_inc(sem["t_rog"], 1)

                    def proj_og(og):
                        k_og(og)
                        r_og(og)

                    def ffn_gu(hg):
                        ti = 24 * f + hg
                        te.wait_ge(*tile_done("wgt", ti))
                        if ti >= 1:
                            te.wait_ge(sem["s_sil"], ti)
                        for gc in range(G):
                            ins = te.matmul(
                                ps_gate, wgt[ti % 2][:, gc * D:(gc + 1) * D],
                                gsl(xn_buf[f % 2], gc),
                                start=(gc == 0), stop=(gc == G - 1))
                        ins.then_inc(sem["t_g"], 1)
                        te.wait_ge(*tile_done("wut", ti))
                        if ti >= 1:
                            te.wait_ge(sem["s_upev"], ti)
                        for gc in range(G):
                            ins = te.matmul(
                                ps_up, wut[ti % 2][:, gc * D:(gc + 1) * D],
                                gsl(xn_buf[f % 2], gc),
                                start=(gc == 0), stop=(gc == G - 1))
                        ins.then_inc(sem["t_u"], 1)

                    if j == 0:
                        stats_a(0)
                        stats_b(0)
                        bc_s(0)
                        bc_m(0)
                    if j < NCH and f < 0:
                        for og in range(G):
                            proj_og(og)
                    if f >= 0:
                        ffn_gu(0)
                        if j < NCH:
                            for og in range(G):
                                k_og(og)
                                r_og(og)
                                if og >= 2:
                                    ffn_gu(og - 1)
                        for hg in range(7 if j < NCH else 1, HG):
                            ffn_gu(hg)
                        for hg in range(HG):
                            if hg == 0:
                                te.wait_ge(sem["v_var"], min(f + 2, NCH))
                                te.wait_ge(sem["s_msev"], min(f + 2, NCH))
                                if f >= 1:
                                    te.wait_ge(sem["v_res2"], f)
                            te.wait_ge(sem["v_hact"], 24 * f + hg + 1)
                            ti48 = 48 * f + hg
                            te.wait_ge(*tile_done("wdt", ti48))
                            for dg in range(4):
                                ins = te.matmul(
                                    ps_dn(dg),
                                    wdt[ti48 % 2][:, dg * D:(dg + 1) * D],
                                    h_act[:, hg * TC:(hg + 1) * TC],
                                    start=(hg == 0), stop=(hg == HG - 1))
                            ins.then_inc(sem["t_d1"], 1)
                        te.wait_ge(sem["v_res1"], f + 1)
                        for hg in range(HG):
                            ti48 = 48 * f + 24 + hg
                            te.wait_ge(*tile_done("wdt", ti48))
                            for dg in range(4, 8):
                                ins = te.matmul(
                                    ps_dn(dg), wdt[ti48 % 2][:, dg * D:(dg + 1) * D],
                                    h_act[:, hg * TC:(hg + 1) * TC],
                                    start=(hg == 0), stop=(hg == HG - 1))
                            ins.then_inc(sem["t_d1"], 1)
                    def wo_cg(cg):
                        cgidx = 8 * j + cg
                        if cg == 0:
                            te.wait_ge(sem["v_out"], 8 * j + 8)
                            if j == 0:
                                te.wait_ge(sem["g_wo"], 16)
                        if cgidx >= 1:
                            te.wait_ge(sem["v_res"], cgidx)
                        for hg in range(G):
                            ins = te.matmul(
                                ps_wo(cg),
                                wo_bf[:, hg * 1024 + cg * D:hg * 1024 + (cg + 1) * D],
                                gsl(outpre, hg),
                                start=(hg == 0), stop=(hg == G - 1))
                        ins.then_inc(sem["t_wo"], 1)

                    if j < NCH:
                        wo_cg(0)
                        wo_cg(1)
                        if j + 1 < NCH:
                            stats_a(j + 1)
                        wo_cg(2)
                        wo_cg(3)
                        if j + 1 < NCH:
                            stats_b(j + 1)
                        wo_cg(4)
                        if j + 1 < NCH:
                            bc_s(j + 1)
                        wo_cg(5)
                        wo_cg(6)
                        if j + 1 < NCH:
                            bc_m(j + 1)
                        wo_cg(7)

        sA.close()
    return nc


def kernel(L0_attn_x, L0_attn_Wk, L0_attn_Wr, L0_attn_Wo,
           L0_attn_time_decay, L0_attn_time_first,
           L0_ffn_w_gate, L0_ffn_w_up, L0_ffn_w_down):
    x = np.asarray(L0_attn_x, dtype=np.float32)
    wkT = np.ascontiguousarray(np.asarray(L0_attn_Wk, dtype=np.float32).T)
    wrT = np.ascontiguousarray(np.asarray(L0_attn_Wr, dtype=np.float32).T)
    woT = np.ascontiguousarray(np.asarray(L0_attn_Wo, dtype=np.float32).T)
    td = np.ascontiguousarray(np.asarray(L0_attn_time_decay, dtype=np.float32).reshape(G, D).T)
    tf = np.ascontiguousarray(np.asarray(L0_attn_time_first, dtype=np.float32).reshape(G, D).T)
    wgT = np.ascontiguousarray(np.asarray(L0_ffn_w_gate, dtype=np.int32).T)
    wuT = np.ascontiguousarray(np.asarray(L0_ffn_w_up, dtype=np.int32).T)
    wdT = np.ascontiguousarray(np.asarray(L0_ffn_w_down, dtype=np.int32).T)

    nc = build_nc()
    in_maps = []
    for b in range(NCORE):
        in_maps.append({
            "x_cm": np.ascontiguousarray(x[b].T),
            "wkT": wkT, "wrT": wrT, "woT": woT,
            "td_in": td, "tf_in": tf,
            "wgT": wgT, "wuT": wuT, "wdT": wdT,
        })
    res = run_bass_kernel_spmd(nc, in_maps, core_ids=list(range(NCORE)))
    out = np.stack([np.ascontiguousarray(r["out_cm"].T) for r in res.results])
    return out
